# version 07 (final)
# baseline (speedup 1.0000x reference)
import math
import contextlib
from collections import defaultdict
import numpy as np
import concourse.bass as bass
import concourse.mybir as mybir

F32 = mybir.dt.float32
BF16 = mybir.dt.bfloat16
I32 = mybir.dt.int32
I16 = mybir.dt.int16
AF = mybir.ActivationFunctionType
ALU = mybir.AluOpType
AX = mybir.AxisListType

S, D, NH, DH = 4096, 4096, 32, 128
QC = 4352
SLAB_A, SLAB_B = 2048, 4096
NCH_A, NCH_B = 16, 32
SCALE = float(DH) ** -0.5
NEG = -1e30

def alibi_slopes(n=NH):
    cp2 = 2 ** math.floor(math.log2(n))
    base = 2.0 ** (-2.0 ** (-(math.log2(cp2) - 3)))
    return (base ** np.arange(1, cp2 + 1, dtype=np.float64)).astype(np.float32)

SLOPES = alibi_slopes()

def nsub_for_head(h):
    return 4 if h < 4 else (2 if h < 8 else 1)

def _has_neg(kk):
    if isinstance(kk, tuple):
        return any(isinstance(x, int) and x < 0 for x in kk) or any(
            isinstance(x, tuple) and _has_neg(x) for x in kk)
    return False


class K:
    def __init__(self):
        self.events = {}
        self.counting = True
        self.engine = None
        self.eng = None
        self.ctr = {}
        self.sems = {}
        self.hist = defaultdict(list)
        self.evsem = {}
        self.lane_rr = 0
        self.glane_rr = 0

    def start_pass(self, name=None, obj=None):
        self.counting = name is None
        self.engine = name
        self.eng = obj
        self.ctr = {k: 0 for k in self.sems}
        self.hist = defaultdict(list)
        self.lane_rr = 0
        self.glane_rr = 0

    def use(self, res, period, newkey):
        h = self.hist[res]
        prev = h[-period] if len(h) >= period else None
        h.append(newkey)
        return prev

    def op(self, eng, fn, waits=(), inc=None, key=None, inc_n=1):
        if self.counting:
            if inc is not None:
                self.ctr[inc] += inc_n
                if key is not None:
                    self.events[key] = self.ctr[inc]
                    self.evsem[key] = inc
            return
        if self.engine != eng:
            return
        for semname, kk in waits:
            if kk is None:
                continue
            if isinstance(kk, int):
                val = kk
            else:
                if _has_neg(kk):
                    continue
                assert kk in self.events, f"missing event {kk}"
                val = self.events[kk]
                if semname == "EV":
                    semname = self.evsem[kk]
            self.eng.wait_ge(self.sems[semname], val)
        inst = fn(self.eng)
        if inc is not None:
            if inst is not None:
                inst.then_inc(self.sems[inc], inc_n)
            else:
                self.eng.sem_inc(self.sems[inc], inc_n)


def build(debug=(), with_yh=True, upto="P6", no_attn=False, force_slow=False, force_mask=False, yh_level=30):
    nc = bass.Bass("TRN2", target_bir_lowering=False)

    xa = nc.dram_tensor("xa", [SLAB_A, D], F32, kind="ExternalInput").ap()
    xb = nc.dram_tensor("xb", [SLAB_B, D], F32, kind="ExternalInput").ap()
    wqkv_sh = nc.dram_tensor("wqkv_sh", [512, QC], F32, kind="ExternalInput").ap()
    wo_sh = nc.dram_tensor("wo_sh", [512, D], F32, kind="ExternalInput").ap()
    qw_sh = nc.dram_tensor("qw_sh", [8, 512, 128], I32, kind="ExternalInput").ap()
    lut4_sh = nc.dram_tensor("lut4_sh", [512, 16], F32, kind="ExternalInput").ap()
    lut8_sh = nc.dram_tensor("lut8_sh", [512, 256], F32, kind="ExternalInput").ap()
    fbias = nc.dram_tensor("fbias", [128, 32], F32, kind="ExternalInput").ap()
    jma = nc.dram_tensor("jma", [128, NCH_A], F32, kind="ExternalInput").ap()
    jmb = nc.dram_tensor("jmb", [128, NCH_B], F32, kind="ExternalInput").ap()

    out = nc.dram_tensor("out", [1024, D], F32, kind="ExternalOutput").ap()

    agw_in = nc.dram_tensor("agw_in", [512, QC + D], BF16).ap()
    agw_out = nc.dram_tensor("agw_out", [8, 512, QC + D], BF16).ap()
    agl_in = nc.dram_tensor("agl_in", [32, 4, 128, 128], BF16).ap()
    agl_out = nc.dram_tensor("agl_out", [8, 32, 4, 128, 128], BF16).ap()
    xmid_d = nc.dram_tensor("xmid_d", [1024, D], F32).ap()
    agf_in = nc.dram_tensor("agf_in", [1, 2], F32).ap()
    flagi_d = nc.dram_tensor("flagi_d", [1, 2], I32).ap()
    agf_out = nc.dram_tensor("agf_out", [1, 2], F32).ap()
    agh_in = nc.dram_tensor("agh_in", [32, 4, 128, 128], BF16).ap()
    agh_out = nc.dram_tensor("agh_out", [8, 32, 4, 128, 128], BF16).ap()

    dbg = {}
    def dbg_t(name, shape, dt):
        if name in debug:
            dbg[name] = nc.dram_tensor("d_" + name, shape, dt, kind="ExternalOutput").ap()
    dbg_t("lnxb", [128, D], BF16)
    dbg_t("ktb", [128, SLAB_B], BF16)
    dbg_t("vb", [128, NCH_B * 128], BF16)
    dbg_t("qt", [128, 512], BF16)
    dbg_t("qt0", [128, 512], BF16)
    dbg_t("qtall", [NH, 128, 512], BF16)
    dbg_t("att", [128, NH * 512], BF16)
    dbg_t("xmid", [1024, D], F32)
    dbg_t("wl", [D, 512], BF16)
    dbg_t("idxf", [128, D], F32)
    dbg_t("dqw", [128, D], BF16)
    dbg_t("mcol", [128, 8], F32)
    dbg_t("den", [128, 512], F32)
    dbg_t("flag", [1, 8], F32)
    dbg_t("ydfix", [128, 1024], BF16)
    dbg_t("ywhr", [128, D], BF16)
    dbg_t("yidx", [128, D], F32)
    dbg_t("ywh", [D, 512], BF16)
    dbg_t("yq8", [128, 1024], I32)
    dbg_t("ylut", [128, 256], F32)
    dbg_t("ypmm", [128, 512], F32)
    dbg_t("probs", [128, 512], BF16)
    dbg_t("scores", [128, 512], F32)

    es = contextlib.ExitStack()
    nc._es = es
    def sb(name, shape, dt): return es.enter_context(nc.sbuf_tensor(name, shape, dt))
    def psum(name, shape, dt=F32): return es.enter_context(nc.psum_tensor(name, shape, dt))

    ARENA_B = 188416
    arena = sb("arena", [128, ARENA_B // 2], BF16)
    SZ = {F32: 4, BF16: 2, I32: 4, I16: 2}
    def carve(off, cols, dt):
        nb = cols * SZ[dt]
        assert off % 64 == 0 and off + nb <= ARENA_B, (off, cols, dt)
        ap = arena[:, off // 2: (off + nb) // 2]
        if dt is not BF16:
            ap = ap.bitcast(dt)
        return ap

    wkv_sb = carve(0, 8192, BF16)
    kta = carve(16384, 2048, BF16)
    va = carve(20480, 2048, BF16)
    ktb = carve(24576, 4096, BF16)
    vb = carve(32768, 4096, BF16)
    lnt = carve(40960, 16384, BF16)
    qsrc = [carve(73728, 16384, BF16), carve(106496, 16384, BF16)]
    xg = [carve(139264, 4096, F32), carve(155648, 4096, F32)]
    lnxr = [carve(172032, 4096, BF16), carve(180224, 4096, BF16)]
    ln2t = carve(0, 32768, BF16)
    attnv = [carve(40960, 16384, BF16), carve(73728, 16384, BF16)]
    obig = carve(106496, 32768, BF16)
    dq_idxf = carve(40960, 4096, F32)
    dq_w = carve(57344, 4096, BF16)
    dq_q = carve(65536, 512, I32)
    dq_pack = [carve(67584, 512, BF16), carve(68608, 512, BF16)]
    dq_prod = carve(73728, 16384, BF16)
    dq_acc = carve(106496, 16384, I32)
    xr = carve(73728, 4096, F32)
    cast_in = carve(73728, QC, F32)
    cast_in2 = carve(106496, 4096, F32)
    cast_out = carve(139264, QC + D, BF16)
    y_idxf = carve(73728, 4096, F32)
    y_whr = carve(90112, 4096, BF16)
    y_dfix = carve(98304, 1024, BF16)
    y_tmp = carve(100352, 128, F32)
    y_obuf = carve(100864, 128, F32)
    y_onew = carve(101376, 128, F32)
    y_xm = carve(101888, 128, F32)
    y_acc = carve(139264, 8192, I32)
    y_prod = carve(139264, 16384, BF16)
    y_q8 = carve(172032, 1024, I32)
    y_lut8 = carve(176128, 256, F32)
    y_pack = [carve(177152, 512, BF16), carve(178176, 512, BF16)]

    qt_sm = [sb(f"qt_sm{i}", [128, 512], BF16)[:] for i in range(2)]
    cols = sb("cols", [128, 24], F32)[:]
    mcol = sb("mcol", [128, 8], F32)[:]
    wt = [sb(f"wt{i}", [128, 128], BF16)[:] for i in range(8)]
    probs = [sb(f"probs{i}", [128, 512], BF16)[:] for i in range(2)]
    rden = sb("rden", [1, 512], F32)[:]
    ones_row = sb("ones_row", [1, 128], BF16)[:]
    rden_hi = sb("rden_hi", [1, 512], BF16)[:]
    rden_lo = sb("rden_lo", [1, 512], BF16)[:]
    bcast = sb("bcast", [128, 512], F32)[:]
    biasc = [sb(f"biasc{i}", [128, NCH_B], F32)[:] for i in range(2)]
    kpos = sb("kpos", [128, NCH_B], F32)[:]
    mbig = sb("mbig", [128, 1152], F32)[:]
    ident = sb("ident", [128, 128], BF16)[:]
    ones1 = sb("ones1", [128, 1], BF16)[:]
    fb_sb = sb("fb_sb", [128, 32], F32)[:]
    jm_sb = sb("jm_sb", [128, NCH_A + NCH_B], F32)[:]
    ostage = [sb(f"ostage{i}", [128, 512], F32)[:] for i in range(2)]
    dq_lut = sb("dq_lut", [128, 16], F32)[:]
    mcb = sb("mcb", [128, 8], BF16)[:]
    flagv = sb("flagv", [1, 4], F32)[:]
    flagi = sb("flagi", [1, 2], I32)[:]
    flagi2 = sb("flagi2", [1, 2], I32)[:]

    p_s = [psum(f"p_s{i}", [128, 512]) for i in range(2)]
    p_pv = psum("p_pv", [128, 512])
    p_den = psum("p_den", [128, 512])
    p_mm = [psum(f"p_mm{i}", [128, 512]) for i in range(2)]
    p_tr = psum("p_tr", [128, 512], BF16)
    p_tr2 = psum("p_tr2", [128, 512], BF16)

    k = K()
    for sn in ["ld", "ld2", "gd", "v", "s", "t", "g", "cc", "outd",
               "yv", "ys", "yt", "yg", "ycc"] + \
            [f"L{i}" for i in range(8)] + [f"G{i}" for i in range(8)] + \
            [f"Y{i}" for i in range(4)] + [f"YG{i}" for i in range(4)]:
        k.sems[sn] = es.enter_context(nc.semaphore("s_" + sn))

    def wq_tile(dc, col0, n=128):
        gg, rr = dc // 4, (dc % 4) * 128
        return agw_out[gg, rr:rr+128, col0:col0+n]
    def wo_tile(dc, oc):
        gg, rr = dc // 4, (dc % 4) * 128
        c0 = QC + oc * 128
        return agw_out[gg, rr:rr+128, c0:c0+128]

    PH_ORDER = ["P0", "P1", "P23", "P4", "DQ", "P5", "P6"]
    def live(ph):
        return PH_ORDER.index(ph) <= PH_ORDER.index(upto)

    def finish(k):
        if k.counting:
            for cn in ["ld", "ld2", "gd", "outd"] + [f"L{i}" for i in range(8)] + [f"G{i}" for i in range(8)]:
                k.events["end_" + cn] = k.ctr[cn]
        k.op("gpsimd", lambda g: None,
             waits=[("gd", "end_gd"), ("outd", "end_outd")] +
                   [(f"G{i}", f"end_G{i}") for i in range(8)])
        k.op("sync", lambda e: None,
             waits=[("ld", "end_ld")] + [(f"L{i}", f"end_L{i}") for i in range(8)])
        k.op("scalar", lambda e: None, waits=[("ld2", "end_ld2")])

    def program(k: K):
        op = k.op
        ev = k.events
        wt_seq = [0]

        def ldma(out_ap, in_ap, waits=(), key=None):
            lane = f"L{k.lane_rr % 8}"
            k.lane_rr += 1
            op("sync", lambda e, out_ap=out_ap, in_ap=in_ap: e.dma_start(
                out=out_ap, in_=in_ap), waits=waits, inc=lane, inc_n=16, key=key)

        def gdma(out_ap, in_ap, waits=(), key=None):
            lane = f"G{k.glane_rr % 8}"
            k.glane_rr += 1
            op("gpsimd", lambda e, out_ap=out_ap, in_ap=in_ap: e.dma_start(
                out=out_ap, in_=in_ap), waits=waits, inc=lane, inc_n=16, key=key)

        def wt_load(src_ap, extra_waits=()):
            n = wt_seq[0]; wt_seq[0] += 1
            wi = n % 8
            w = list(extra_waits)
            if n >= 8:
                w.append(("t", ("wtfree", n - 8)))
            ldma(wt[wi], src_ap, waits=w, key=("wld", n))
            return n, wi

        op("gpsimd", lambda g: g.iota(kpos, pattern=[[128, NCH_B]], base=0,
                                      channel_multiplier=1, allow_small_or_imprecise_dtypes=True))
        op("gpsimd", lambda g: g.iota(mbig, pattern=[[-1, 1152]], base=512,
                                      channel_multiplier=1, allow_small_or_imprecise_dtypes=True))
        op("gpsimd", lambda g: g.iota(ident, pattern=[[-1, 128]], base=0,
                                      channel_multiplier=1, allow_small_or_imprecise_dtypes=True),
           inc="g", key="consts_g")
        def cfn(v):
            v.tensor_scalar(out=mbig, in0=mbig, scalar1=0.0, scalar2=NEG,
                            op0=ALU.is_gt, op1=ALU.mult)
            v.tensor_scalar(out=ident, in0=ident, scalar1=0.0, scalar2=1.0,
                            op0=ALU.is_equal, op1=ALU.mult)
            v.memset(ones1, 1.0)
            v.memset(ones_row, 1.0)
            v.drain()
        op("vector", cfn, waits=[("g", "consts_g")], inc="v", key="consts_v")

        op("sync", lambda e: e.dma_start(out=jm_sb[:, 0:NCH_A], in_=jma), inc="ld", inc_n=16)
        op("sync", lambda e: e.dma_start(out=jm_sb[:, NCH_A:], in_=jmb), inc="ld", inc_n=16)
        op("sync", lambda e: e.dma_start(out=fb_sb, in_=fbias), inc="ld", inc_n=16,
           key="smalls_ld")

        for r in range(4):
            w = [("v", ("castv", r - 1))] if r > 0 else []
            op("sync", lambda e, r=r: e.dma_start(out=cast_in, in_=wqkv_sh[r*128:(r+1)*128, :]),
               waits=w, inc="ld", inc_n=16)
            op("sync", lambda e, r=r: e.dma_start(out=cast_in2, in_=wo_sh[r*128:(r+1)*128, :]),
               inc="ld", inc_n=16, key=("castld", r))
            def cast_fn(v):
                v.tensor_copy(out=cast_out[:, 0:QC], in_=cast_in)
                v.tensor_copy(out=cast_out[:, QC:], in_=cast_in2)
                v.drain()
            w2 = [("ld", ("castld", r))]
            if r > 0:
                w2.append(("gd", ("castst", r - 1)))
            op("vector", cast_fn, waits=w2, inc="v", key=("castv", r))
            op("gpsimd", lambda g, r=r: g.dma_start(out=agw_in[r*128:(r+1)*128, :],
                                                    in_=cast_out),
               waits=[("v", ("castv", r))], inc="gd", inc_n=16, key=("castst", r))
        op("gpsimd", lambda g: g.collective_compute(
            "AllGather", ALU.bypass, replica_groups=[list(range(8))],
            ins=[agw_in], outs=[agw_out]),
           waits=[("gd", ("castst", 3))], inc="cc", key="ag1")

        for dc in range(32):
            op("scalar", lambda e, dc=dc: e.dma_start(out=wkv_sb[:, dc*256:dc*256+128],
                                                      in_=wq_tile(dc, D)),
               waits=([("cc", "ag1")] if dc == 0 else []), inc="ld2", inc_n=16)
            op("scalar", lambda e, dc=dc: e.dma_start(out=wkv_sb[:, dc*256+128:dc*256+256],
                                                      in_=wq_tile(dc, D + DH)),
               inc="ld2", inc_n=16, key=("wkvld", dc))

        def ln_group(tag, g, src_ap, extra_xld_waits=(), ln2_group=None):
            ring = len(k.hist["xg"]) % 2
            b = 8 * ring
            prev_lnv = k.use("xg", 2, (tag, "lnv", g))
            prev_tr = k.use("lnxr", 2, (tag, "trlast", g))
            w0 = list(extra_xld_waits)
            if prev_lnv is not None:
                w0.append(("v", prev_lnv))
            lane = f"L{k.lane_rr % 8}"
            k.lane_rr += 1
            op("sync", lambda e: e.dma_start(out=xg[ring], in_=src_ap),
               waits=w0, inc=lane, inc_n=16, key=(tag, "xld", g))
            def v1(v):
                v.reduce_sum(out=cols[:, b:b+1], in_=xg[ring], axis=AX.X)
                v.drain()
                v.tensor_scalar(out=cols[:, b+1:b+2], in0=cols[:, b:b+1],
                                scalar1=1.0/D, scalar2=0.0, op0=ALU.mult, op1=ALU.add)
                v.drain()
            op("vector", v1, waits=[("EV", (tag, "xld", g))], inc="v", key=(tag, "mean", g))
            wsq = [("v", (tag, "mean", g))]
            if prev_tr is not None:
                wsq.append(("t", prev_tr))
            op("scalar", lambda s: s.activation(out=lnxr[ring], in_=xg[ring], func=AF.Square,
                                                bias=cols[:, b+1:b+2], scale=-1.0,
                                                accum_out=cols[:, b+2:b+3]),
               waits=wsq, inc="s", key=(tag, "sq", g))
            def v2(v):
                v.tensor_scalar(out=cols[:, b+3:b+4], in0=cols[:, b+2:b+3],
                                scalar1=1.0/D, scalar2=1e-5, op0=ALU.mult, op1=ALU.add)
                v.drain()
                v.reciprocal(out=cols[:, b+4:b+5], in_=cols[:, b+3:b+4])
                v.drain()
            op("vector", v2, waits=[("s", (tag, "sq", g))], inc="v", key=(tag, "rv", g))
            op("scalar", lambda s: s.activation(out=cols[:, b+5:b+6], in_=cols[:, b+4:b+5],
                                                func=AF.Sqrt, bias=0.0, scale=1.0),
               waits=[("v", (tag, "rv", g))], inc="s", key=(tag, "rs", g))
            if ln2_group is not None:
                gg = ln2_group
                def v3(v):
                    v.tensor_tensor(out=cols[:, b+6:b+7], in0=cols[:, b+2:b+3],
                                    in1=cols[:, b+4:b+5], op=ALU.mult)
                    v.drain()
                op("vector", v3, waits=[("s", (tag, "rs", g))], inc="v", key=(tag, "e1", g))
                op("scalar", lambda s: s.activation(out=cols[:, b+6:b+7], in_=cols[:, b+6:b+7],
                                                    func=AF.Sqrt, bias=0.0, scale=1.0),
                   waits=[("v", (tag, "e1", g))], inc="s", key=(tag, "e2", g))
                def v4(v):
                    v.tensor_scalar(out=mcol[:, gg:gg+1], in0=cols[:, b+6:b+7],
                                    scalar1=64.0, scalar2=1.0, op0=ALU.is_gt, op1=ALU.mult)
                    v.drain()
                op("vector", v4, waits=[("s", (tag, "e2", g))], inc="v", key=(tag, "mask", g))
            def v5(v):
                v.tensor_scalar(out=lnxr[ring], in0=xg[ring],
                                scalar1=cols[:, b+1:b+2], scalar2=cols[:, b+5:b+6],
                                op0=ALU.subtract, op1=ALU.mult)
                v.drain()
            op("vector", v5, waits=[("s", (tag, "rs", g))], inc="v", key=(tag, "lnv", g))
            return ring

        def tr_group(tag, g, ring, dest, gl, tokw, extra_evac_waits=(),
                     extra_tr_waits=()):
            dap = dest.rearrange("p (c t) -> p c t", t=tokw)
            for dcq in range(8):
                bank = p_tr if dcq % 2 == 0 else p_tr2
                for i in range(4):
                    dc = dcq * 4 + i
                    w = []
                    if i == 0:
                        w.append(("v", (tag, "lnv", g)))
                        if dcq < 2:
                            w.extend(extra_tr_waits)
                        if dcq >= 2:
                            w.append(("s", (tag, "trev", g, dcq - 2)))
                    kk = (tag, "trlast", g) if (dcq == 7 and i == 3) else (tag, "tr", g, dcq, i)
                    op("tensor", lambda t, bank=bank, i=i, dc=dc, ring=ring: t.transpose(
                        bank[:, i*128:(i+1)*128], lnxr[ring][:, dc*128:(dc+1)*128], ident),
                       waits=w, inc="t", key=kk)
                dst = dap[:, dcq*4:dcq*4+4, gl*128:(gl+1)*128]
                srcv = bank[:].rearrange("p (c t) -> p c t", t=128)
                wk = (tag, "trlast", g) if dcq == 7 else (tag, "tr", g, dcq, 3)
                w2 = [("t", wk)] + list(extra_evac_waits)
                op("scalar", lambda s, dst=dst, srcv=srcv: s.copy(out=dst, in_=srcv),
                   waits=w2, inc="s", key=(tag, "trev", g, dcq))
            if k.counting:
                ev[(tag, "trdone", g)] = ev[(tag, "trev", g, 7)]

        if not live("P1"):
            return finish(k)
        for sname, xsrc, slen, ktdst, vdst, tix in (
                ("A", xa, SLAB_A, kta, va, 0), ("B", xb, SLAB_B, ktb, vb, 1)):
            tag = ("P1", sname)
            nsg = slen // 512
            for sg in range(nsg):
                own = (sg == nsg - 1)
                dest = qsrc[tix] if own else lnt
                for gl in range(4):
                    g = sg * 4 + gl
                    exw = []
                    if sname == "A" and g < 2:
                        exw.append(("gd", ("castst", 3)))
                    ring = ln_group(tag, g, xsrc[g*128:(g+1)*128, :], extra_xld_waits=exw)
                    eev = []
                    if not own and sg > 0:
                        eev.append(("t", (tag, "vmm", sg - 1, 3, 31)))
                    if own:
                        eev.append(("gd", ("castst", 3)))
                    tr_group(tag, g, ring, dest, gl, 512, extra_evac_waits=eev)
                src = dest
                for dc in range(32):
                    w = []
                    if dc == 0:
                        w = [("s", (tag, "trdone", sg * 4 + 3)), ("ld2", ("wkvld", 31)),
                             ("s", (tag, "ktev", sg - 1))]
                        if sname == "B":
                            w.append(("s", (("P1", "A"), "ktev", SLAB_A // 512 - 1)))
                    op("tensor", lambda t, dc=dc, src=src: t.matmul(
                        p_mm[0][:], wkv_sb[:, dc*256:dc*256+128], src[:, dc*512:(dc+1)*512],
                        start=(dc == 0), stop=(dc == 31)),
                       waits=w, inc="t", key=(tag, "ktmm", sg, dc))
                op("scalar", lambda s, sg=sg, ktdst=ktdst: s.copy(
                    out=ktdst[:, sg*512:(sg+1)*512], in_=p_mm[0][:]),
                   waits=[("t", (tag, "ktmm", sg, 31))], inc="s", key=(tag, "ktev", sg))
                for tg in range(4):
                    for dc in range(32):
                        w = []
                        if dc == 0:
                            w = [("s", (tag, "vev", sg, tg - 1)),
                                 ("s", (tag, "vev", sg - 1, 3))]
                            if sname == "B" and sg == 0 and tg == 0:
                                w.append(("s", (("P1", "A"), "vev", SLAB_A // 512 - 1, 3)))
                        op("tensor", lambda t, dc=dc, tg=tg, src=src: t.matmul(
                            p_mm[1][:, 0:128], src[:, dc*512+tg*128:dc*512+tg*128+128],
                            wkv_sb[:, dc*256+128:dc*256+256],
                            start=(dc == 0), stop=(dc == 31)),
                           waits=w, inc="t", key=(tag, "vmm", sg, tg, dc))
                    op("scalar", lambda s, sg=sg, tg=tg, vdst=vdst: s.copy(
                        out=vdst[:, (sg*4+tg)*128:(sg*4+tg)*128+128], in_=p_mm[1][:, 0:128]),
                       waits=[("t", (tag, "vmm", sg, tg, 31))], inc="s",
                       key=(tag, "vev", sg, tg))

        if "lnxb" in dbg:
            op("gpsimd", lambda g: g.dma_start(out=dbg["lnxb"], in_=lnxr[1]),
               waits=[("v", (("P1", "B"), "lnv", 31))], inc="gd", inc_n=16)
        if "ktb" in dbg:
            op("gpsimd", lambda g: g.dma_start(out=dbg["ktb"], in_=ktb),
               waits=[("s", (("P1", "B"), "ktev", 7))], inc="gd", inc_n=16)
        if "vb" in dbg:
            op("gpsimd", lambda g: g.dma_start(out=dbg["vb"], in_=vb),
               waits=[("s", (("P1", "B"), "vev", 7, 3))], inc="gd", inc_n=16)

        if not live("P23"):
            return finish(k)
        for tix, (sname, slen, nch, ktsrc, vsrc) in enumerate(
                (("A", SLAB_A, NCH_A, kta, va), ("B", SLAB_B, NCH_B, ktb, vb))):
            ptag = ("P2", sname)
            atag = ("P3", sname)
            nsg = slen // 512
            slab = slen
            bias_i = 0
            prev_nc = [0]
            last_mm1_of_head = {}
            for h in range(NH):
                loads = []
                for dc in range(32):
                    ew = []
                    if h == 0 and dc == 0:
                        ew.append(("cc", "ag1"))
                        if sname == "B":
                            ew.append(("v", (("P3", "A"), "done")))
                    loads.append(wt_load(wq_tile(dc, h * 128), extra_waits=ew))
                pm = p_mm[h % 2]
                for dc in range(32):
                    n, wi = loads[dc]
                    w = [("EV", ("wld", n))]
                    if dc == 0:
                        w.append(("s", (ptag, "qev", h - 2)))
                        w.append(("s", (("P1", sname), "trdone", slen // 128 - 1)))
                    op("tensor", lambda t, pm=pm, wi=wi, dc=dc, tix=tix: t.matmul(
                        pm[:], wt[wi], qsrc[tix][:, dc*512:(dc+1)*512],
                        start=(dc == 0), stop=(dc == 31)),
                       waits=w, inc="t", key=("wtfree", n))
                    if k.counting:
                        ev[(ptag, "qmm", h, dc)] = ev[("wtfree", n)]
                wq_ev = [("t", (ptag, "qmm", h, 31))]
                if h >= 2 and not no_attn:
                    wq_ev.append(("t", last_mm1_of_head[h - 2]))
                if "qtall" in dbg and h >= 2:
                    wq_ev.append(("gd", (ptag, "qtd", h - 2)))
                op("scalar", lambda s, h=h, pm=pm: s.activation(
                    out=qt_sm[h % 2], in_=pm[:], func=AF.Copy, bias=0.0, scale=SCALE),
                   waits=wq_ev, inc="s", key=(ptag, "qev", h))
                if "qtall" in dbg and sname == "B":
                    op("gpsimd", lambda g, h=h: g.dma_start(
                        out=dbg["qtall"][h], in_=qt_sm[h % 2]),
                       waits=[("s", (ptag, "qev", h))], inc="gd", inc_n=16,
                       key=(ptag, "qtd", h))
                elif k.counting and "qtall" in dbg:
                    k.ctr["gd"] += 16
                    ev[(ptag, "qtd", h)] = k.ctr["gd"]
                elif "qtall" in dbg:
                    op("gpsimd", lambda g, h=h: g.dma_start(
                        out=dbg["qtall"][h], in_=qt_sm[h % 2]),
                       waits=[("s", (ptag, "qev", h))], inc="gd", inc_n=16,
                       key=(ptag, "qtd", h))
                if no_attn:
                    continue
                ns = nsub_for_head(h)
                wq = 512 // ns
                slope = float(SLOPES[h])
                for sub in range(ns):
                    bi = bias_i % 2
                    chunks = [ch for ch in range(nch)
                              if ch * 128 <= slab - 512 + (sub + 1) * wq - 1
                              and slope * ((slab - 512 + sub * wq)
                                           - (ch * 128 + 127)) < 55.0]
                    nc_ = len(chunks)
                    prev_mm3_last = (atag, "mm3", bias_i - 1, prev_nc[0] - 1) \
                        if bias_i > 0 else None
                    prev_nc[0] = nc_
                    Cs = slab - 512 + sub * wq + wq / 2.0 + 16.0
                    joff = 0 if sname == "A" else NCH_A
                    def biasfn(v, slope=slope, Cs=Cs, bi=bi, nch=nch, joff=joff):
                        v.tensor_scalar(out=biasc[bi][:, 0:nch], in0=kpos[:, 0:nch],
                                        scalar1=slope, scalar2=-slope * Cs,
                                        op0=ALU.mult, op1=ALU.add)
                        v.drain()
                        v.tensor_tensor(out=biasc[bi][:, 0:nch], in0=biasc[bi][:, 0:nch],
                                        in1=jm_sb[:, joff:joff+nch], op=ALU.add)
                        v.drain()
                    wb = [("s", (atag, "explast", bias_i - 2)), ("v", "consts_v"),
                          ("ld", "smalls_ld")]
                    op("vector", biasfn, waits=wb, inc="v", key=(atag, "bias", bias_i))
                    for ci, ch in enumerate(chunks):
                        cs = ch * 128
                        pb = p_s[ci % 2]
                        w = [("s", (atag, "exp", bias_i, ci - 1)), ("s", (ptag, "qev", h))]
                        if ci == 1:
                            w.append(("s", (atag, "explast", bias_i - 1)))
                        if ci == 0:
                            w.append(("v", (atag, "pvev", bias_i - 1)))
                            if bias_i >= 1:
                                w.append(("s", (atag, "bcev", bias_i - 1)))

                            if bias_i == 0:
                                w.append(("s", (("P1", sname), "ktev", nsg - 1)))
                                w.append(("s", (("P1", sname), "vev", nsg - 1, 3)))

                        op("tensor", lambda t, pb=pb, cs=cs, h=h, sub=sub, wq=wq,
                           ktsrc=ktsrc: t.matmul(
                            pb[:, 0:wq], ktsrc[:, cs:cs+128],
                            qt_sm[h % 2][:, sub*wq: sub*wq+wq],
                            start=True, stop=True),
                           waits=w, inc="t", key=(atag, "mm1", bias_i, ci))
                        last_mm1_of_head[h] = (atag, "mm1", bias_i, ci)
                        diag = cs >= slab - 512
                        if diag:
                            off = slab + sub * wq - cs
                            def mfn(v, pb=pb, off=off, wq=wq):
                                v.tensor_tensor(out=pb[:, 0:wq], in0=pb[:, 0:wq],
                                                in1=mbig[:, off:off+wq], op=ALU.add)
                                v.drain()
                            op("vector", mfn, waits=[("t", (atag, "mm1", bias_i, ci))],
                               inc="v", key=(atag, "madd", bias_i, ci))
                        we = [("t", (atag, "mm1", bias_i, ci)),
                              ("t", (atag, "mm3", bias_i, ci - 1)),
                              ("v", (atag, "bias", bias_i))]
                        if ci < 2 and prev_mm3_last is not None:
                            we.append(("t", prev_mm3_last))
                        if diag:
                            we.append(("v", (atag, "madd", bias_i, ci)))
                        op("scalar", lambda s, pb=pb, ch=ch, bi=bi, wq=wq, ci=ci: s.activation(
                            out=probs[ci % 2][:, 0:wq], in_=pb[:, 0:wq], func=AF.Exp,
                            bias=biasc[bi][:, ch:ch+1], scale=1.0),
                           waits=we, inc="s", key=(atag, "exp", bias_i, ci))
                        w2 = [("s", (atag, "exp", bias_i, ci))]
                        if ci == 0:
                            w2.append(("v", (atag, "pvev", bias_i - 1)))
                        op("tensor", lambda t, ch=ch, ci=ci, wq=wq, vsrc=vsrc,
                           first=(ci == 0), last=(ci == nc_ - 1): t.matmul(
                            p_pv[:, 0:wq], vsrc[:, ch*128:(ch+1)*128],
                            probs[ci % 2][:, 0:wq], start=first, stop=last),
                           waits=w2, inc="t", key=(atag, "mm2", bias_i, ci))
                        op("tensor", lambda t, ci=ci, wq=wq,
                           first=(ci == 0), last=(ci == nc_ - 1): t.matmul(
                            p_den[0:1, 0:wq], ones1, probs[ci % 2][:, 0:wq],
                            start=first, stop=last),
                           inc="t", key=(atag, "mm3", bias_i, ci))
                    if k.counting:
                        ev[(atag, "explast", bias_i)] = ev[(atag, "exp", bias_i, nc_ - 1)]
                    def rfn(v, wq=wq):
                        v.reciprocal(out=rden[:, 0:wq], in_=p_den[0:1, 0:wq])
                        v.drain()
                        v.tensor_copy(out=rden_hi[:, 0:wq], in_=rden[:, 0:wq])
                        v.drain()
                        v.tensor_tensor(out=rden_lo[:, 0:wq], in0=rden[:, 0:wq],
                                        in1=rden_hi[:, 0:wq], op=ALU.subtract)
                        v.drain()
                    op("vector", rfn, waits=[("t", (atag, "mm3", bias_i, nc_ - 1)),
                                             ("t", (atag, "bc", bias_i - 1))],
                       inc="v", key=(atag, "recip", bias_i))
                    op("tensor", lambda t, wq=wq: t.matmul(
                        p_s[0][:, 0:wq], ones_row, rden_hi[:, 0:wq],
                        start=True, stop=False),
                       waits=[("v", (atag, "recip", bias_i)),
                              ("s", (atag, "explast", bias_i))],
                       inc="t")
                    op("tensor", lambda t, wq=wq: t.matmul(
                        p_s[0][:, 0:wq], ones_row, rden_lo[:, 0:wq],
                        start=False, stop=True),
                       inc="t", key=(atag, "bc", bias_i))
                    op("scalar", lambda s, wq=wq: s.copy(out=bcast[:, 0:wq],
                                                         in_=p_s[0][:, 0:wq]),
                       waits=[("t", (atag, "bc", bias_i)),
                              ("v", (atag, "pvev", bias_i - 1))], inc="s",
                       key=(atag, "bcev", bias_i))
                    def dfn(v, h=h, sub=sub, wq=wq, tix=tix):
                        v.tensor_tensor(out=attnv[tix][:, h*512+sub*wq: h*512+sub*wq+wq],
                                        in0=p_pv[:, 0:wq], in1=bcast[:, 0:wq], op=ALU.mult)
                        v.drain()
                    op("vector", dfn,
                       waits=[("s", (atag, "bcev", bias_i)),
                              ("t", (atag, "mm2", bias_i, nc_ - 1))],
                       inc="v", key=(atag, "pvev", bias_i))
                    bias_i += 1
            if no_attn:
                op("vector", lambda v: None, waits=[("s", (ptag, "qev", NH - 1))],
                   inc="v", key=(atag, "done2"))
            if k.counting:
                ev[(atag, "done")] = ev[(atag, "pvev", bias_i - 1)] if not no_attn \
                    else ev[(atag, "done2")]
            if "qt" in dbg and sname == "B":
                op("gpsimd", lambda g: g.dma_start(out=dbg["qt"], in_=qt_sm[1]),
                   waits=[("v", (atag, "done"))], inc="gd", inc_n=16)
            if "qt0" in dbg and sname == "B":
                op("gpsimd", lambda g: g.dma_start(out=dbg["qt0"], in_=qt_sm[0]),
                   waits=[("v", (atag, "done"))], inc="gd", inc_n=16)
            if "probs" in dbg and sname == "B":
                op("gpsimd", lambda g: g.dma_start(out=dbg["probs"], in_=probs[1]),
                   waits=[("v", (atag, "done"))], inc="gd", inc_n=16)
            if "scores" in dbg and sname == "B":
                op("gpsimd", lambda g: g.dma_start(out=dbg["scores"], in_=p_s[1][:]),
                   waits=[("v", (atag, "done"))], inc="gd", inc_n=16)
            if "den" in dbg and sname == "B":
                op("gpsimd", lambda g: g.dma_start(out=dbg["den"], in_=bcast),
                   waits=[("v", (atag, "done"))], inc="gd", inc_n=16)
            if "att" in dbg and sname == "B":
                op("gpsimd", lambda g, tix=tix: g.dma_start(out=dbg["att"], in_=attnv[tix]),
                   waits=[("v", (atag, "done"))], inc="gd", inc_n=16)

        if not live("P4"):
            return finish(k)
        t4 = "P4"
        for oc in range(32):
            loads = []
            for dc in range(32):
                ew = []
                if oc == 0 and dc == 0:
                    ew += [("v", (("P3", "A"), "done")), ("v", (("P3", "B"), "done"))]
                loads.append(wt_load(wo_tile(dc, oc), extra_waits=ew))
            for dc in range(32):
                n, wi = loads[dc]
                w = [("EV", ("wld", n))]
                if dc == 0:
                    w.append(("s", (t4, "ev", oc - 1, 1)))
                op("tensor", lambda t, wi=wi, dc=dc: t.matmul(
                    p_mm[0][:], wt[wi], attnv[0][:, dc*512:(dc+1)*512],
                    start=(dc == 0), stop=(dc == 31)),
                   waits=w, inc="t")
                op("tensor", lambda t, wi=wi, dc=dc: t.matmul(
                    p_mm[1][:], wt[wi], attnv[1][:, dc*512:(dc+1)*512],
                    start=(dc == 0), stop=(dc == 31)),
                   inc="t", key=("wtfree", n))
                if k.counting:
                    ev[(t4, "mm", oc, dc)] = ev[("wtfree", n)]
            for half in range(2):
                op("scalar", lambda s, oc=oc, half=half: s.copy(
                    out=obig[:, oc*1024+half*512: oc*1024+half*512+512], in_=p_mm[half][:]),
                   waits=[("t", (t4, "mm", oc, 31))], inc="s", key=(t4, "ev", oc, half))
        for tokg in range(8):
            if tokg < 4:
                xsrc = xa[1536 + tokg*128: 1536 + tokg*128 + 128, :]
            else:
                xsrc = xb[3584 + (tokg-4)*128: 3584 + (tokg-4)*128 + 128, :]
            ldma(xr, xsrc, waits=[("v", (t4, "addv", tokg - 1, 7)),
                                  ("s", (t4, "ev", 31, 1))], key=(t4, "xrld", tokg))
            for ocg in range(8):
                bank = p_tr if ocg % 2 == 0 else p_tr2
                for i in range(4):
                    oc = ocg * 4 + i
                    w = []
                    if i == 0:
                        w = [("s", (t4, "ev", 31, 1)),
                             ("v", (t4, "addv", tokg, ocg - 2)),
                             ("v", (t4, "addv", tokg - 1, 7))]
                    op("tensor", lambda t, bank=bank, i=i, oc=oc, tokg=tokg: t.transpose(
                        bank[:, i*128:(i+1)*128],
                        obig[:, oc*1024+tokg*128: oc*1024+tokg*128+128], ident),
                       waits=w, inc="t", key=(t4, "tr", tokg, ocg, i))
                def addfn(v, bank=bank, ocg=ocg):
                    v.tensor_tensor(out=ostage[ocg % 2], in0=bank[:],
                                    in1=xr[:, ocg*512:(ocg+1)*512], op=ALU.add)
                    v.drain()
                op("vector", addfn,
                   waits=[("t", (t4, "tr", tokg, ocg, 3)), ("EV", (t4, "xrld", tokg)),
                          ("EV", (t4, "xst", tokg, ocg - 2)),
                          ("EV", (t4, "xst", tokg - 1, 7))],
                   inc="v", key=(t4, "addv", tokg, ocg))
                gdma(xmid_d[tokg*128:(tokg+1)*128, ocg*512:(ocg+1)*512],
                     ostage[ocg % 2],
                     waits=[("v", (t4, "addv", tokg, ocg))],
                     key=(t4, "xst", tokg, ocg))
        if k.counting:
            ev[(t4, "xmid_done")] = ev[(t4, "xst", 7, 7)]
            k.evsem[(t4, "xmid_done")] = k.evsem[(t4, "xst", 7, 7)]
            ev[(t4, "tok_done")] = ev[(t4, "addv", 7, 7)]
        if "xmid" in dbg:
            op("gpsimd", lambda g: g.dma_start(out=dbg["xmid"], in_=xmid_d),
               waits=[("EV", (t4, "xmid_done"))], inc="gd", inc_n=16)

        if not live("DQ"):
            return finish(k)
        def dequant(planes, lutsrc, lutbuf, nclass, agdst, tag):
            npl = len(planes)
            for ot in range(4):
                for pi, p in enumerate(planes):
                    w = [("v", (tag, "idxf", ot - 1))] if pi == 0 else []
                    if ot == 0 and pi == 0:
                        w.append(("s", (t4, "ev", 31, 1)))
                        w.append(("v", (t4, "tok_done")))
                    ldma(dq_q[:, pi*128:(pi+1)*128], qw_sh[p, ot*128:(ot+1)*128, :],
                         waits=w)
                ldma(lutbuf[:, 0:nclass], lutsrc[ot*128:(ot+1)*128, :],
                     waits=[("v", (tag, "prod", ot - 1, 3))],
                     key=(tag, "dqld", ot))
                def unp(v, npl=npl):
                    for pi in range(npl):
                        for bbit in range(32):
                            shf = bbit - pi
                            dst = dq_acc[:, pi*D + bbit: pi*D + bbit + 127*32 + 1: 32]
                            if shf >= 0:
                                v.tensor_scalar(out=dst, in0=dq_q[:, pi*128:(pi+1)*128],
                                                scalar1=shf, scalar2=(1 << pi),
                                                op0=ALU.logical_shift_right, op1=ALU.bitwise_and)
                            else:
                                v.tensor_scalar(out=dst, in0=dq_q[:, pi*128:(pi+1)*128],
                                                scalar1=-shf, scalar2=(1 << pi),
                                                op0=ALU.logical_shift_left, op1=ALU.bitwise_and)
                    v.drain()
                    v.tensor_tensor(out=dq_acc[:, 0:D], in0=dq_acc[:, 0:D],
                                    in1=dq_acc[:, D:2*D], op=ALU.bitwise_or)
                    v.tensor_tensor(out=dq_acc[:, 2*D:3*D], in0=dq_acc[:, 2*D:3*D],
                                    in1=dq_acc[:, 3*D:4*D], op=ALU.bitwise_or)
                    v.drain()
                    v.tensor_tensor(out=dq_acc[:, 0:D], in0=dq_acc[:, 0:D],
                                    in1=dq_acc[:, 2*D:3*D], op=ALU.bitwise_or)
                    v.drain()
                    v.tensor_copy(out=dq_idxf, in_=dq_acc[:, 0:D])
                    v.drain()
                wunp = [("EV", (tag, "dqld", ot)), ("t", (tag, "trp", ot - 1, 7))]
                if ot == 0:
                    wunp.append(("v", (t4, "tok_done")))
                op("vector", unp, waits=wunp, inc="v", key=(tag, "idxf", ot))
                for icq in range(4):
                    def prod(v, icq=icq, nclass=nclass, lutbuf=lutbuf):
                        for c in range(nclass):
                            v.tensor_scalar(out=dq_prod[:, c*1024:(c+1)*1024],
                                            in0=dq_idxf[:, icq*1024:(icq+1)*1024],
                                            scalar1=float(c), scalar2=lutbuf[:, c:c+1],
                                            op0=ALU.is_equal, op1=ALU.mult)
                        v.drain()
                        step = 1
                        while step < nclass:
                            for c in range(0, nclass, 2 * step):
                                v.tensor_tensor(out=dq_prod[:, c*1024:(c+1)*1024],
                                                in0=dq_prod[:, c*1024:(c+1)*1024],
                                                in1=dq_prod[:, (c+step)*1024:(c+step+1)*1024],
                                                op=ALU.add)
                            v.drain()
                            step *= 2
                        v.tensor_copy(out=dq_w[:, icq*1024:(icq+1)*1024],
                                      in_=dq_prod[:, 0:1024])
                        v.drain()
                    op("vector", prod, waits=[("v", (tag, "idxf", ot))],
                       inc="v", key=(tag, "prod", ot, icq))
                for icq in range(8):
                    bank = p_tr if icq % 2 == 0 else p_tr2
                    for i in range(4):
                        ic = icq * 4 + i
                        w = []
                        if i == 0:
                            w.append(("v", (tag, "prod", ot, ic // 8)))
                            if icq >= 2:
                                w.append(("s", (tag, "trev", ot, icq - 2)))
                        op("tensor", lambda t, bank=bank, i=i, ic=ic: t.transpose(
                            bank[:, i*128:(i+1)*128], dq_w[:, ic*128:(ic+1)*128], ident),
                           waits=w, inc="t",
                           key=((tag, "trp", ot, icq) if i == 3 else None))
                    op("scalar", lambda s, bank=bank, icq=icq: s.copy(
                        out=dq_pack[icq % 2], in_=bank[:]),
                       waits=[("t", (tag, "trp", ot, icq)),
                              ("EV", (tag, "wst", ot, icq - 2))],
                       inc="s", key=(tag, "trev", ot, icq))
                    gdma(agdst[icq*4:(icq+1)*4, ot]
                         .rearrange("c p o -> p c o"),
                         dq_pack[icq % 2].rearrange("p (c o) -> p c o", o=128),
                         waits=[("s", (tag, "trev", ot, icq))],
                         key=(tag, "wst", ot, icq))
            if k.counting:
                ev[(tag, "done")] = ev[(tag, "wst", 3, 7)]
                ev[(tag, "vdone")] = ev[(tag, "prod", 3, 3)]

        dequant([0, 1, 2, 3], lut4_sh, dq_lut, 16, agl_in, ("DQ", "l"))
        if "dqw" in dbg:
            op("gpsimd", lambda g: g.dma_start(out=dbg["dqw"], in_=dq_w),
               waits=[("v", (("DQ", "l"), "prod", 3, 3)),
                      ("t", (("DQ", "l"), "trp", 3, 7))], inc="gd", inc_n=16)
        if "idxf" in dbg:
            op("gpsimd", lambda g: g.dma_start(out=dbg["idxf"], in_=dq_idxf),
               waits=[("v", (("DQ", "l"), "prod", 3, 3))], inc="gd", inc_n=16)
        if k.counting:
            for i in range(8):
                ev[("ag2g", i)] = k.ctr[f"G{i}"]
        op("gpsimd", lambda g: g.collective_compute(
            "AllGather", ALU.bypass, replica_groups=[list(range(8))],
            ins=[agl_in], outs=[agl_out]),
           waits=[(f"G{i}", ("ag2g", i)) for i in range(8)], inc="cc", key="ag2")
        if "wl" in dbg:
            op("gpsimd", lambda g: g.dma_start(out=dbg["wl"], in_=agl_in),
               waits=[(f"G{i}", ("ag2g", i)) for i in range(8)], inc="gd", inc_n=16)

        if not live("P5"):
            return finish(k)
        t5 = "P5"
        for g in range(8):
            exw = [("EV", (t4, "xst", g, 7))]
            if g < 2:
                exw.append(("v", (("DQ", "l"), "idxf", 3)))
            ring = ln_group(t5, g, xmid_d[g*128:(g+1)*128, :],
                            extra_xld_waits=exw, ln2_group=g)
            eev = []
            etw = []
            if g == 0:
                eev = [("t", (("DQ", "l"), "trp", 3, 7))]
                etw = [("s", (("DQ", "l"), "trev", 3, 7)),
                       ("s", (("DQ", "l"), "trev", 3, 6))]
            tr_group(t5, g, ring, ln2t, g, 1024, extra_evac_waits=eev,
                     extra_tr_waits=etw)
        if "mcol" in dbg:
            op("gpsimd", lambda gg: gg.dma_start(out=dbg["mcol"], in_=mcol),
               waits=[("v", (t5, "mask", 7))], inc="gd", inc_n=16)

        if not live("P6"):
            return finish(k)
        t6 = "P6"
        for oc in range(32):
            gsh, csh = oc // 4, (oc % 4) * 128
            pmb = p_mm if oc % 2 == 0 else p_s
            loads = []
            for ic in range(32):
                ew = []
                if oc == 0 and ic == 0:
                    ew += [("cc", "ag2"), ("s", (t5, "trdone", 7))]
                loads.append(wt_load(agl_out[gsh, ic, csh // 128],
                                     extra_waits=ew))
            for ic in range(32):
                n, wi = loads[ic]
                w = [("EV", ("wld", n))]
                if ic == 0:
                    w.append(("EV", (t6, "ev", oc - 2, 1)))
                op("tensor", lambda t, pmb=pmb, wi=wi, ic=ic: t.matmul(
                    pmb[0][:], wt[wi], ln2t[:, ic*1024: ic*1024+512],
                    start=(ic == 0), stop=(ic == 31)),
                   waits=w, inc="t")
                op("tensor", lambda t, pmb=pmb, wi=wi, ic=ic: t.matmul(
                    pmb[1][:], wt[wi], ln2t[:, ic*1024+512: ic*1024+1024],
                    start=(ic == 0), stop=(ic == 31)),
                   inc="t", key=("wtfree", n))
                if k.counting:
                    ev[(t6, "mm", oc, ic)] = ev[("wtfree", n)]
            for half in range(2):
                def fev(v, oc=oc, half=half, pmb=pmb):
                    v.tensor_scalar(out=obig[:, oc*1024+half*512: oc*1024+half*512+512],
                                    in0=pmb[half][:], scalar1=fb_sb[:, oc:oc+1],
                                    scalar2=0.0, op0=ALU.add, op1=ALU.add)
                    v.drain()
                op("vector", fev, waits=[("t", (t6, "mm", oc, 31))],
                   inc="v", key=(t6, "ev", oc, half))
        for tokg in range(8):
            ldma(xr, xmid_d[tokg*128:(tokg+1)*128, :],
                 waits=[("v", (t6, "addv", tokg - 1, 7)), ("EV", (t6, "ev", 31, 1)),
                        ("EV", (t4, "xmid_done"))],
                 key=(t6, "xrld", tokg))
            for ocg in range(8):
                bank = p_tr if ocg % 2 == 0 else p_tr2
                for i in range(4):
                    oc = ocg * 4 + i
                    w = []
                    if i == 0:
                        w = [("EV", (t6, "ev", 31, 1)),
                             ("v", (t6, "addv", tokg, ocg - 2)),
                             ("v", (t6, "addv", tokg - 1, 7))]
                    op("tensor", lambda t, bank=bank, i=i, oc=oc, tokg=tokg: t.transpose(
                        bank[:, i*128:(i+1)*128],
                        obig[:, oc*1024+tokg*128: oc*1024+tokg*128+128], ident),
                       waits=w, inc="t", key=(t6, "tr", tokg, ocg, i))
                def addfn(v, bank=bank, ocg=ocg):
                    v.tensor_tensor(out=ostage[ocg % 2], in0=bank[:],
                                    in1=xr[:, ocg*512:(ocg+1)*512], op=ALU.add)
                    v.drain()
                op("vector", addfn,
                   waits=[("t", (t6, "tr", tokg, ocg, 3)), ("EV", (t6, "xrld", tokg)),
                          ("EV", (t6, "ost", tokg, ocg - 2)),
                          ("EV", (t6, "ost", tokg - 1, 7))],
                   inc="v", key=(t6, "addv", tokg, ocg))
                gdma(out[tokg*128:(tokg+1)*128, ocg*512:(ocg+1)*512],
                     ostage[ocg % 2],
                     waits=[("v", (t6, "addv", tokg, ocg))],
                     key=(t6, "ost", tokg, ocg))

        ty = "YH"
        if force_mask:
            op("vector", lambda v: (v.memset(mcol, 1.0), v.drain(), None)[2],
               waits=[("v", (t5, "mask", 7))], inc="v", key="fmask")
        op("vector", lambda v: (v.tensor_copy(out=mcb, in_=mcol), v.drain(), None)[2],
           waits=[("v", (t5, "mask", 7))] + ([("v", "fmask")] if force_mask else []),
           inc="v", key="mcb")
        op("tensor", lambda t: t.matmul(p_den[0:1, 0:8], ones1, mcb,
                                        start=True, stop=True),
           waits=[("v", "mcb"), ("v", (("P3", "B"), "done"))], inc="t", key="fmm")
        def ffl(v):
            v.reduce_sum(out=flagv[:, 0:1], in_=p_den[0:1, 0:8], axis=AX.X)
            v.drain()
            v.tensor_scalar(out=flagv[:, 1:2], in0=flagv[:, 0:1], scalar1=0.0,
                            scalar2=1.0, op0=ALU.is_gt, op1=ALU.mult)
            v.drain()
        op("vector", ffl, waits=[("t", "fmm")], inc="v", key="fv1")
        op("gpsimd", lambda g: g.dma_start(out=agf_in, in_=flagv[0:1, 0:2]),
           waits=[("v", "fv1")], inc="gd", inc_n=16, key="fst")
        op("gpsimd", lambda g: g.collective_compute(
            "AllReduce", ALU.max, replica_groups=[list(range(8))],
            ins=[agf_in], outs=[agf_out]),
           waits=[("gd", "fst")], inc="cc", key="frd")
        op("gpsimd", lambda g: g.dma_start(out=flagv[0:1, 2:4], in_=agf_out),
           waits=[("cc", "frd")], inc="gd", inc_n=16, key="fld")
        def ffi(v):
            v.tensor_scalar(out=flagi[:, 0:1], in0=flagv[:, 3:4], scalar1=0.0,
                            scalar2=1.0, op0=ALU.is_gt, op1=ALU.mult)
            v.drain()
            if force_slow:
                v.memset(flagi[:, 0:1], 1)
                v.drain()
        op("vector", ffi, waits=[("gd", "fld")], inc="v", key="flagi")
        op("gpsimd", lambda g: g.dma_start(out=flagi_d, in_=flagi[0:1, 0:2]),
           waits=[("v", "flagi")], inc="gd", inc_n=16, key="fst2")
        ldma(flagi2[0:1, 0:2], flagi_d, waits=[("gd", "fst2")], key="fli")
        if "flag" in dbg:
            def fdump(v):
                v.tensor_copy(out=flagv[:, 2:3], in_=flagv[:, 2:3])
            op("gpsimd", lambda g: g.dma_start(out=dbg["flag"][0:1, 0:4], in_=flagv[0:1, 0:4]),
               waits=[("v", "flagi")], inc="gd", inc_n=16)
            op("gpsimd", lambda g: g.dma_start(
                out=dbg["flag"][0:1, 4:6], in_=flagi[0:1, 0:2].bitcast(F32)),
               inc="gd", inc_n=16)

        ylane = [0]
        def ydma(out_ap, in_ap, waits=(), key=None):
            lane = f"Y{ylane[0] % 4}"
            ylane[0] += 1
            op("sync", lambda e, out_ap=out_ap, in_ap=in_ap: e.dma_start(
                out=out_ap, in_=in_ap), waits=waits, inc=lane, inc_n=16, key=key)
        yglane = [0]
        def ygdma(out_ap, in_ap, waits=(), key=None):
            lane = f"YG{yglane[0] % 4}"
            yglane[0] += 1
            op("gpsimd", lambda e, out_ap=out_ap, in_ap=in_ap: e.dma_start(
                out=out_ap, in_=in_ap), waits=waits, inc=lane, inc_n=16, key=key)

        def slow_section():
            if k.counting:
                for i in range(8):
                    ev[("yhg", i)] = k.ctr[f"G{i}"]
            if yh_level < 2:
                op("vector", lambda v: (v.memset(y_tmp, 0.0), v.drain(), None)[2],
                   inc="yv", key=(ty, "triv"))
                return
            def dequant_loop():
              for ot in range(4):
                  w0 = [(f"G{i}", ("yhg", i)) for i in range(8)] if ot == 0 else \
                       ([("yv", (ty, "red", ot - 1, 7))] if yh_level >= 22
                        else [("yv", (ty, "idx", ot - 1, 3))])
                  for p in range(8):
                      ydma(y_q8[:, p*128:(p+1)*128], qw_sh[p, ot*128:(ot+1)*128, :],
                           waits=(w0 if p == 0 else ()), key=(ty, "qld", ot, p))
                  ydma(y_lut8, lut8_sh[ot*128:(ot+1)*128, :], waits=w0, key=(ty, "ld", ot))
                  for ic4 in range(4):
                      def unp8(v, ic4=ic4):
                          for p in range(8):
                              for b in range(32):
                                  dst = y_acc[:, p*1024 + b: p*1024 + b + 31*32 + 1: 32]
                                  shf = b - p
                                  if shf >= 0:
                                      v.tensor_scalar(out=dst,
                                                      in0=y_q8[:, p*128+ic4*32: p*128+ic4*32+32],
                                                      scalar1=shf, scalar2=(1 << p),
                                                      op0=ALU.logical_shift_right,
                                                      op1=ALU.bitwise_and)
                                  else:
                                      v.tensor_scalar(out=dst,
                                                      in0=y_q8[:, p*128+ic4*32: p*128+ic4*32+32],
                                                      scalar1=-shf, scalar2=(1 << p),
                                                      op0=ALU.logical_shift_left,
                                                      op1=ALU.bitwise_and)
                          v.drain()
                          for st in (1, 2, 4):
                              for c in range(0, 8, 2 * st):
                                  v.tensor_tensor(out=y_acc[:, c*1024:(c+1)*1024],
                                                  in0=y_acc[:, c*1024:(c+1)*1024],
                                                  in1=y_acc[:, (c+st)*1024:(c+st+1)*1024],
                                                  op=ALU.bitwise_or)
                              v.drain()
                          v.tensor_copy(out=y_idxf[:, ic4*1024:(ic4+1)*1024],
                                        in_=y_acc[:, 0:1024])
                          v.drain()
                      wu = [("EV", (ty, "ld", ot))] + \
                           [("EV", (ty, "qld", ot, p)) for p in range(8)]
                      if ot > 0 and ic4 == 0 and yh_level >= 22:
                          wu.append(("yv", (ty, "red", ot - 1, 7)))
                      op("vector", unp8, waits=wu, inc="yv", key=(ty, "idx", ot, ic4))
                  if yh_level < 22:
                      continue
                  for ic4 in range(4):
                      def prod256(v, ic4=ic4, ot=ot):
                          sl = slice(ic4*1024, (ic4+1)*1024)
                          v.memset(y_whr[:, sl], 0.0)
                          v.drain()
                          for grp in range(16):
                              for j in range(16):
                                  c = grp * 16 + j
                                  v.tensor_scalar(out=y_prod[:, j*1024:(j+1)*1024],
                                                  in0=y_idxf[:, sl],
                                                  scalar1=float(c),
                                                  scalar2=y_lut8[:, c:c+1],
                                                  op0=ALU.is_equal, op1=ALU.mult)
                              v.drain()
                              for st in (1, 2, 4, 8):
                                  for cc in range(0, 16, 2 * st):
                                      v.tensor_tensor(
                                          out=y_prod[:, cc*1024:(cc+1)*1024],
                                          in0=y_prod[:, cc*1024:(cc+1)*1024],
                                          in1=y_prod[:, (cc+st)*1024:(cc+st+1)*1024],
                                          op=ALU.add)
                                  v.drain()
                              v.tensor_tensor(out=y_whr[:, sl], in0=y_whr[:, sl],
                                              in1=y_prod[:, 0:1024], op=ALU.add)
                              v.drain()
                      op("vector", prod256, waits=[("yv", (ty, "idx", ot, 3))],
                         inc="yv", key=(ty, "red", ot, 2 * ic4))
                      if k.counting:
                          ev[(ty, "red", ot, 2 * ic4 + 1)] = ev[(ty, "red", ot, 2 * ic4)]
                  if yh_level < 23:
                      continue
                  for icq in range(8):
                      bank = p_tr if icq % 2 == 0 else p_tr2
                      for i in range(4):
                          ic = icq * 4 + i
                          w = []
                          if i == 0:
                              w.append(("yv", (ty, "red", ot, (ic * 128) // 512)))
                              w.append(("yv", (ty, "red", ot, min(7, ((ic + 3) * 128) // 512))))
                              if icq >= 2:
                                  w.append(("ys", (ty, "trev", ot, icq - 2)))
                              if ot == 0 and icq < 2:
                                  w.append(("v", (t6, "addv", 7, 7)))
                          op("tensor", lambda t, bank=bank, i=i, ic=ic: t.transpose(
                              bank[:, i*128:(i+1)*128], y_whr[:, ic*128:(ic+1)*128], ident),
                             waits=w, inc="yt",
                             key=((ty, "trp", ot, icq) if i == 3 else None))
                      op("scalar", lambda s, bank=bank, icq=icq: s.copy(
                          out=y_pack[icq % 2], in_=bank[:]),
                         waits=[("yt", (ty, "trp", ot, icq)),
                                ("EV", (ty, "wst", ot, icq - 2))],
                         inc="ys", key=(ty, "trev", ot, icq))
                      ygdma(agh_in[icq*4:(icq+1)*4, ot]
                            .rearrange("c p o -> p c o"),
                            y_pack[icq % 2].rearrange("p (c o) -> p c o", o=128),
                            waits=[("ys", (ty, "trev", ot, icq))],
                            key=(ty, "wst", ot, icq))
            if k.counting:
                m0 = k.ctr["yv"]
                dequant_loop()
                ev["yvd"] = k.ctr["yv"] - m0
            elif k.engine == "vector":
                k.eng.wait_ge(k.sems[k.evsem["fli"]], ev["fli"])
                vregs = nc.alloc_registers("yfr_v", engines=[k.eng.engine])
                k.eng.reg_load(vregs[k.eng.engine], flagi2[0:1, 0:1])
                with nc.If_ne(vregs, 0):
                    dequant_loop()
                with nc.Else():
                    k.eng.sem_inc(k.sems["yv"], ev["yvd"])
            else:
                dequant_loop()
            if "yq8" in dbg:
                op("gpsimd", lambda g: g.dma_start(out=dbg["yq8"], in_=y_q8),
                   waits=[("EV", (ty, "qld", 3, p)) for p in range(8)] +
                         [("yv", (ty, "idx", 3, 3))], inc="gd", inc_n=16)
            if "yidx" in dbg:
                op("gpsimd", lambda g: g.dma_start(out=dbg["yidx"], in_=y_idxf),
                   waits=[("yv", (ty, "idx", 3, 3))], inc="gd", inc_n=16)
            if "ywhr" in dbg and yh_level >= 22:
                op("gpsimd", lambda g: g.dma_start(out=dbg["ywhr"], in_=y_whr),
                   waits=[("yv", (ty, "red", 3, 6))], inc="gd", inc_n=16)
            if "ylut" in dbg:
                op("gpsimd", lambda g: g.dma_start(out=dbg["ylut"], in_=y_lut8),
                   waits=[("yv", (ty, "red", 3, 6))], inc="gd", inc_n=16)
            if k.counting:
                for i in range(4):
                    ev[("yag", i)] = k.ctr[f"YG{i}"]
            if yh_level < 25:
                op("gpsimd", lambda g: None,
                   waits=[(f"YG{i}", ("yag", i)) for i in range(4)])
                return
            if "ywh" in dbg:
                op("gpsimd", lambda g: g.dma_start(out=dbg["ywh"], in_=agh_in),
                   waits=[(f"YG{i}", ("yag", i)) for i in range(4)], inc="gd", inc_n=16)
            op("gpsimd", lambda g: g.collective_compute(
                "AllGather", ALU.bypass, replica_groups=[list(range(8))],
                ins=[agh_in], outs=[agh_out]),
               waits=[(f"YG{i}", ("yag", i)) for i in range(4)], inc="ycc", key="ag3")
            if yh_level < 30:
                return
            ywt = [0]
            def ymm_loop():
              for oc in range(32):
                  gsh, csh = oc // 4, (oc % 4) * 128
                  pmb = p_mm if oc % 2 == 0 else p_s
                  lds = []
                  for ic in range(32):
                      n = ywt[0]; ywt[0] += 1
                      wi = n % 8
                      w = [("yt", (ty, "wfree", n - 8))] if n >= 8 else [("ycc", "ag3")]
                      ydma(wt[wi], agh_out[gsh, ic, csh // 128],
                           waits=w, key=(ty, "wld", n))
                      lds.append((n, wi))
                  for ic in range(32):
                      n, wi = lds[ic]
                      w = [("EV", (ty, "wld", n))]
                      if ic == 0:
                          w.append(("yv", (ty, "dfx", oc - 2, 1)))
                      op("tensor", lambda t, pmb=pmb, wi=wi, ic=ic: t.matmul(
                          pmb[0][:], wt[wi], ln2t[:, ic*1024: ic*1024+512],
                          start=(ic == 0), stop=(ic == 31)),
                         waits=w, inc="yt")
                      op("tensor", lambda t, pmb=pmb, wi=wi, ic=ic: t.matmul(
                          pmb[1][:], wt[wi], ln2t[:, ic*1024+512: ic*1024+1024],
                          start=(ic == 0), stop=(ic == 31)),
                         inc="yt", key=(ty, "wfree", n))
                      if k.counting:
                          ev[(ty, "ymm", oc, ic)] = ev[(ty, "wfree", n)]
                  for half in range(2):
                      def dfx(v, oc=oc, half=half, pmb=pmb):
                          v.tensor_scalar(out=y_dfix[:, half*512:(half+1)*512],
                                          in0=pmb[half][:], scalar1=fb_sb[:, oc:oc+1],
                                          scalar2=0.0, op0=ALU.add, op1=ALU.add)
                          v.drain()
                      wd = [("yt", (ty, "ymm", oc, 31))]
                      if oc > 0:
                          wd.append(("yt", (ty, "mtr", oc - 1, 7)))
                      op("vector", dfx, waits=wd, inc="yv", key=(ty, "dfx", oc, half))
                  for tokg in range(8):
                      bank = p_tr if tokg % 2 == 0 else p_tr2
                      w = [("yv", (ty, "dfx", oc, tokg // 4)),
                           ("yv", (ty, "dfx", oc, 1)),
                           ("yv", (ty, "mrg", oc, tokg - 2)),
                           ("yv", (ty, "mrg", oc - 1, 7))]
                      op("tensor", lambda t, bank=bank, tokg=tokg: t.transpose(
                          bank[:, 0:128], y_dfix[:, tokg*128:(tokg+1)*128], ident),
                         waits=w, inc="yt", key=(ty, "mtr", oc, tokg))
                      if tokg > 0:
                          wold = [("yv", (ty, "mrg", oc, tokg - 1))]
                      elif oc > 0:
                          wold = [("yv", (ty, "mrg", oc - 1, 7))]
                      else:
                          wold = [(f"G{i}", ("yhg", i)) for i in range(8)]
                      ygdma(y_obuf, out[tokg*128:(tokg+1)*128, oc*128:(oc+1)*128],
                            waits=wold, key=(ty, "old", oc, tokg))
                      ygdma(y_xm, xmid_d[tokg*128:(tokg+1)*128, oc*128:(oc+1)*128],
                            waits=(), key=(ty, "xld", oc, tokg))
                      def mrg(v, bank=bank, tokg=tokg):
                          v.tensor_tensor(out=y_tmp, in0=bank[:, 0:128], in1=y_xm,
                                          op=ALU.add)
                          v.drain()
                          v.tensor_tensor(out=y_tmp, in0=y_tmp, in1=y_obuf,
                                          op=ALU.subtract)
                          v.drain()
                          v.tensor_scalar(out=y_tmp, in0=y_tmp,
                                          scalar1=mcol[:, tokg:tokg+1], scalar2=0.0,
                                          op0=ALU.mult, op1=ALU.add)
                          v.drain()
                          v.tensor_tensor(out=y_onew, in0=y_obuf, in1=y_tmp, op=ALU.add)
                          v.drain()
                      wm = [("yt", (ty, "mtr", oc, tokg)),
                            ("EV", (ty, "old", oc, tokg)),
                            ("EV", (ty, "xld", oc, tokg))]
                      if tokg > 0:
                          wm.append(("EV", (ty, "ost", oc, tokg - 1)))
                      elif oc > 0:
                          wm.append(("EV", (ty, "ost", oc - 1, 7)))
                      op("vector", mrg, waits=wm,
                         inc="yv", key=(ty, "mrg", oc, tokg))
                      ygdma(out[tokg*128:(tokg+1)*128, oc*128:(oc+1)*128], y_onew,
                            waits=[("yv", (ty, "mrg", oc, tokg))],
                            key=(ty, "ost", oc, tokg))
            ymm_loop()
            if "ydfix" in dbg:
                op("gpsimd", lambda g: g.dma_start(out=dbg["ydfix"], in_=y_dfix),
                   waits=[("yv", (ty, "dfx", 31, 1))], inc="gd", inc_n=16)
            if "ypmm" in dbg:
                op("scalar", lambda s: s.copy(out=ostage[0], in_=p_mm[0][:]),
                   waits=[("yv", (ty, "dfx", 31, 1))], inc="ys", key=(ty, "pmmcp"))
                op("gpsimd", lambda g: g.dma_start(out=dbg["ypmm"], in_=ostage[0]),
                   waits=[("ys", (ty, "pmmcp"))], inc="gd", inc_n=16)
            if k.counting:
                for i in range(4):
                    ev[("yend", i)] = k.ctr[f"YG{i}"]
                    ev[("yendl", i)] = k.ctr[f"Y{i}"]
            op("gpsimd", lambda g: None,
               waits=[(f"YG{i}", ("yend", i)) for i in range(4)])
            op("sync", lambda e: None,
               waits=[(f"Y{i}", ("yendl", i)) for i in range(4)])

        if with_yh:
            slow_section()

        return finish(k)

    k.start_pass()
    program(k)
    with nc.Block() as blk:
        @blk.sync
        def _(e):
            k.start_pass("sync", e); program(k)
        @blk.vector
        def _(e):
            k.start_pass("vector", e); program(k)
        @blk.scalar
        def _(e):
            k.start_pass("scalar", e); program(k)
        @blk.tensor
        def _(e):
            k.start_pass("tensor", e); program(k)
        @blk.gpsimd
        def _(e):
            k.start_pass("gpsimd", e); program(k)
    return nc, dbg


def prep_core_inputs(full, core):
    x = full["L0_attn_hidden_states"]
    b = core // 4
    j = core % 4
    qendA = 512 * (j + 1)
    qendB = 512 * (8 - j)
    xbf = x[b]
    def slab(qend, slen):
        lo = qend - slen
        pad = max(0, -lo)
        seg = xbf[max(0, lo):qend]
        if pad:
            seg = np.concatenate([np.zeros((pad, D), np.float32), seg], axis=0)
        return np.ascontiguousarray(seg)
    def jmask(qend, slen):
        nch = slen // 128
        jm = np.zeros((128, nch), np.float32)
        npad = max(0, slen - qend)
        for ch in range(nch):
            pp = ch * 128 + np.arange(128)
            jm[:, ch] = np.where(pp < npad, NEG, 0.0)
        return jm
    r0 = core * 512
    return {
        "xa": slab(qendA, SLAB_A),
        "xb": slab(qendB, SLAB_B),
        "wqkv_sh": np.ascontiguousarray(full["L0_attn_Wqkv"][r0:r0+512]),
        "wo_sh": np.ascontiguousarray(full["L0_attn_Wo"][r0:r0+512]),
        "qw_sh": np.ascontiguousarray(full["L0_ffn_qweight"][:, r0:r0+512, :]),
        "lut4_sh": np.ascontiguousarray(full["L0_ffn_lut4"][r0:r0+512]),
        "lut8_sh": np.ascontiguousarray(full["L0_ffn_lut8"][r0:r0+512]),
        "fbias": np.ascontiguousarray(full["L0_ffn_bias"].reshape(32, 128).T),
        "jma": jmask(qendA, SLAB_A),
        "jmb": jmask(qendB, SLAB_B),
    }


def assemble_out(results):
    y = np.zeros((2, S, D), np.float32)
    for core in range(8):
        b, j = core // 4, core % 4
        o = results[core]["out"]
        y[b, 512*j:512*(j+1)] = o[:512]
        y[b, 512*(7-j):512*(8-j)] = o[512:]
    return y


_BUILD_CACHE = {}

def kernel(**inputs):
    key = "v6"
    if key not in _BUILD_CACHE:
        _BUILD_CACHE[key] = build(upto="P6")
    nc, _ = _BUILD_CACHE[key]
    from concourse.bass_utils import run_bass_kernel_spmd
    in_maps = [prep_core_inputs(inputs, c) for c in range(8)]
    res = run_bass_kernel_spmd(nc, in_maps, core_ids=list(range(8)))
    return assemble_out([res.results[c] for c in range(8)])


# version 08
# speedup vs baseline: 1.0299x; 1.0299x over previous
import math
import contextlib
from collections import defaultdict
import numpy as np
import concourse.bass as bass
import concourse.mybir as mybir

F32 = mybir.dt.float32
BF16 = mybir.dt.bfloat16
I32 = mybir.dt.int32
I16 = mybir.dt.int16
AF = mybir.ActivationFunctionType
ALU = mybir.AluOpType
AX = mybir.AxisListType

S, D, NH, DH = 4096, 4096, 32, 128
QC = 4352
SLAB_A, SLAB_B = 2048, 4096
NCH_A, NCH_B = 16, 32
SCALE = float(DH) ** -0.5
NEG = -1e30

def alibi_slopes(n=NH):
    cp2 = 2 ** math.floor(math.log2(n))
    base = 2.0 ** (-2.0 ** (-(math.log2(cp2) - 3)))
    return (base ** np.arange(1, cp2 + 1, dtype=np.float64)).astype(np.float32)

SLOPES = alibi_slopes()

def nsub_for_head(h):
    return 4 if h < 4 else (2 if h < 8 else 1)

def _has_neg(kk):
    if isinstance(kk, tuple):
        return any(isinstance(x, int) and x < 0 for x in kk) or any(
            isinstance(x, tuple) and _has_neg(x) for x in kk)
    return False


class K:
    def __init__(self):
        self.events = {}
        self.counting = True
        self.engine = None
        self.eng = None
        self.ctr = {}
        self.sems = {}
        self.hist = defaultdict(list)
        self.evsem = {}
        self.lane_rr = 0
        self.glane_rr = 0

    def start_pass(self, name=None, obj=None):
        self.counting = name is None
        self.engine = name
        self.eng = obj
        self.ctr = {k: 0 for k in self.sems}
        self.hist = defaultdict(list)
        self.lane_rr = 0
        self.glane_rr = 0

    def use(self, res, period, newkey):
        h = self.hist[res]
        prev = h[-period] if len(h) >= period else None
        h.append(newkey)
        return prev

    def op(self, eng, fn, waits=(), inc=None, key=None, inc_n=1):
        if self.counting:
            if inc is not None:
                self.ctr[inc] += inc_n
                if key is not None:
                    self.events[key] = self.ctr[inc]
                    self.evsem[key] = inc
            return
        if self.engine != eng:
            return
        for semname, kk in waits:
            if kk is None:
                continue
            if isinstance(kk, int):
                val = kk
            else:
                if _has_neg(kk):
                    continue
                assert kk in self.events, f"missing event {kk}"
                val = self.events[kk]
                if semname == "EV":
                    semname = self.evsem[kk]
            self.eng.wait_ge(self.sems[semname], val)
        inst = fn(self.eng)
        if inc is not None:
            if inst is not None:
                inst.then_inc(self.sems[inc], inc_n)
            else:
                self.eng.sem_inc(self.sems[inc], inc_n)


def build(debug=(), with_yh=True, upto="P6", no_attn=False, force_slow=False, force_mask=False, yh_level=30):
    nc = bass.Bass("TRN2", target_bir_lowering=False)

    xa = nc.dram_tensor("xa", [SLAB_A, D], F32, kind="ExternalInput").ap()
    xb = nc.dram_tensor("xb", [SLAB_B, D], F32, kind="ExternalInput").ap()
    wqkv_sh = nc.dram_tensor("wqkv_sh", [512, QC], F32, kind="ExternalInput").ap()
    wo_sh = nc.dram_tensor("wo_sh", [512, D], F32, kind="ExternalInput").ap()
    qw_sh = nc.dram_tensor("qw_sh", [8, 512, 128], I32, kind="ExternalInput").ap()
    lut4_sh = nc.dram_tensor("lut4_sh", [512, 16], F32, kind="ExternalInput").ap()
    lut8_sh = nc.dram_tensor("lut8_sh", [512, 256], F32, kind="ExternalInput").ap()
    fbias = nc.dram_tensor("fbias", [128, 32], F32, kind="ExternalInput").ap()
    jma = nc.dram_tensor("jma", [128, NCH_A], F32, kind="ExternalInput").ap()
    jmb = nc.dram_tensor("jmb", [128, NCH_B], F32, kind="ExternalInput").ap()

    out = nc.dram_tensor("out", [1024, D], F32, kind="ExternalOutput").ap()

    agw_in = nc.dram_tensor("agw_in", [512, QC + D], BF16).ap()
    agw_out = nc.dram_tensor("agw_out", [8, 512, QC + D], BF16).ap()
    agl_in = nc.dram_tensor("agl_in", [32, 4, 128, 128], BF16).ap()
    agl_out = nc.dram_tensor("agl_out", [8, 32, 4, 128, 128], BF16).ap()
    xmid_d = nc.dram_tensor("xmid_d", [1024, D], F32).ap()
    agf_in = nc.dram_tensor("agf_in", [1, 2], F32).ap()
    flagi_d = nc.dram_tensor("flagi_d", [1, 2], I32).ap()
    agf_out = nc.dram_tensor("agf_out", [1, 2], F32).ap()
    agh_in = nc.dram_tensor("agh_in", [32, 4, 128, 128], BF16).ap()
    agh_out = nc.dram_tensor("agh_out", [8, 32, 4, 128, 128], BF16).ap()

    dbg = {}
    def dbg_t(name, shape, dt):
        if name in debug:
            dbg[name] = nc.dram_tensor("d_" + name, shape, dt, kind="ExternalOutput").ap()
    dbg_t("lnxb", [128, D], BF16)
    dbg_t("ktb", [128, SLAB_B], BF16)
    dbg_t("vb", [128, NCH_B * 128], BF16)
    dbg_t("qt", [128, 512], BF16)
    dbg_t("qt0", [128, 512], BF16)
    dbg_t("qtall", [NH, 128, 512], BF16)
    dbg_t("att", [128, NH * 512], BF16)
    dbg_t("xmid", [1024, D], F32)
    dbg_t("wl", [D, 512], BF16)
    dbg_t("idxf", [128, D], F32)
    dbg_t("dqw", [128, D], BF16)
    dbg_t("mcol", [128, 8], F32)
    dbg_t("den", [128, 512], F32)
    dbg_t("flag", [1, 8], F32)
    dbg_t("ydfix", [128, 1024], BF16)
    dbg_t("ywhr", [128, D], BF16)
    dbg_t("yidx", [128, D], F32)
    dbg_t("ywh", [D, 512], BF16)
    dbg_t("yq8", [128, 1024], I32)
    dbg_t("ylut", [128, 256], F32)
    dbg_t("ypmm", [128, 512], F32)
    dbg_t("probs", [128, 512], BF16)
    dbg_t("scores", [128, 512], F32)

    es = contextlib.ExitStack()
    nc._es = es
    def sb(name, shape, dt): return es.enter_context(nc.sbuf_tensor(name, shape, dt))
    def psum(name, shape, dt=F32): return es.enter_context(nc.psum_tensor(name, shape, dt))

    ARENA_B = 188416
    arena = sb("arena", [128, ARENA_B // 2], BF16)
    SZ = {F32: 4, BF16: 2, I32: 4, I16: 2}
    def carve(off, cols, dt):
        nb = cols * SZ[dt]
        assert off % 64 == 0 and off + nb <= ARENA_B, (off, cols, dt)
        ap = arena[:, off // 2: (off + nb) // 2]
        if dt is not BF16:
            ap = ap.bitcast(dt)
        return ap

    wkv_sb = carve(0, 8192, BF16)
    kta = carve(16384, 2048, BF16)
    va = carve(20480, 2048, BF16)
    ktb = carve(24576, 4096, BF16)
    vb = carve(32768, 4096, BF16)
    lnt = carve(40960, 16384, BF16)
    qsrc = [carve(73728, 16384, BF16), carve(106496, 16384, BF16)]
    xg = [carve(139264, 4096, F32), carve(155648, 4096, F32)]
    lnxr = [carve(172032, 4096, BF16), carve(180224, 4096, BF16)]
    ln2t = carve(0, 32768, BF16)
    attnv = [carve(40960, 16384, BF16), carve(73728, 16384, BF16)]
    obig = carve(106496, 32768, BF16)
    dq_idxf = carve(40960, 4096, F32)
    dq_w = carve(57344, 4096, BF16)
    dq_q = carve(65536, 512, I32)
    dq_pack = [carve(67584, 512, BF16), carve(68608, 512, BF16)]
    dq_prod = carve(73728, 16384, BF16)
    dq_acc = carve(106496, 16384, I32)
    xr = carve(73728, 4096, F32)
    cast_in = carve(73728, QC, F32)
    cast_in2 = carve(106496, 4096, F32)
    cast_out = carve(139264, QC + D, BF16)
    y_idxf = carve(73728, 4096, F32)
    y_whr = carve(90112, 4096, BF16)
    y_dfix = carve(98304, 1024, BF16)
    y_tmp = carve(100352, 128, F32)
    y_obuf = carve(100864, 128, F32)
    y_onew = carve(101376, 128, F32)
    y_xm = carve(101888, 128, F32)
    y_acc = carve(139264, 8192, I32)
    y_prod = carve(139264, 16384, BF16)
    y_q8 = carve(172032, 1024, I32)
    y_lut8 = carve(176128, 256, F32)
    y_pack = [carve(177152, 512, BF16), carve(178176, 512, BF16)]

    qt_sm = [sb(f"qt_sm{i}", [128, 512], BF16)[:] for i in range(2)]
    cols = sb("cols", [128, 24], F32)[:]
    mcol = sb("mcol", [128, 8], F32)[:]
    wt = [sb(f"wt{i}", [128, 128], BF16)[:] for i in range(12)]
    probs = [sb(f"probs{i}", [128, 512], BF16)[:] for i in range(2)]
    rden = sb("rden", [1, 512], F32)[:]
    ones_row = sb("ones_row", [1, 128], BF16)[:]
    rden_hi = sb("rden_hi", [1, 512], BF16)[:]
    rden_lo = sb("rden_lo", [1, 512], BF16)[:]
    bcast = sb("bcast", [128, 512], F32)[:]
    biasc = [sb(f"biasc{i}", [128, NCH_B], F32)[:] for i in range(2)]
    kpos = sb("kpos", [128, NCH_B], F32)[:]
    mbig = sb("mbig", [128, 1152], F32)[:]
    ident = sb("ident", [128, 128], BF16)[:]
    ones1 = sb("ones1", [128, 1], BF16)[:]
    fb_sb = sb("fb_sb", [128, 32], F32)[:]
    jm_sb = sb("jm_sb", [128, NCH_A + NCH_B], F32)[:]
    ostage = [sb(f"ostage{i}", [128, 512], F32)[:] for i in range(2)]
    dq_lut = sb("dq_lut", [128, 16], F32)[:]
    mcb = sb("mcb", [128, 8], BF16)[:]
    flagv = sb("flagv", [1, 4], F32)[:]
    flagi = sb("flagi", [1, 2], I32)[:]
    flagi2 = sb("flagi2", [1, 2], I32)[:]

    p_s = [psum(f"p_s{i}", [128, 512]) for i in range(2)]
    p_pv = psum("p_pv", [128, 512])
    p_den = psum("p_den", [128, 512])
    p_mm = [psum(f"p_mm{i}", [128, 512]) for i in range(2)]
    p_tr = psum("p_tr", [128, 512], BF16)
    p_tr2 = psum("p_tr2", [128, 512], BF16)

    k = K()
    for sn in ["ld", "ld2", "gd", "v", "s", "t", "g", "cc", "outd",
               "yv", "ys", "yt", "yg", "ycc"] + \
            [f"L{i}" for i in range(8)] + [f"G{i}" for i in range(8)] + \
            [f"Y{i}" for i in range(4)] + [f"YG{i}" for i in range(4)]:
        k.sems[sn] = es.enter_context(nc.semaphore("s_" + sn))

    def wq_tile(dc, col0, n=128):
        gg, rr = dc // 4, (dc % 4) * 128
        return agw_out[gg, rr:rr+128, col0:col0+n]
    def wo_tile(dc, oc):
        gg, rr = dc // 4, (dc % 4) * 128
        c0 = QC + oc * 128
        return agw_out[gg, rr:rr+128, c0:c0+128]

    PH_ORDER = ["P0", "P1", "P23", "P4", "DQ", "P5", "P6"]
    def live(ph):
        return PH_ORDER.index(ph) <= PH_ORDER.index(upto)

    def finish(k):
        if k.counting:
            for cn in ["ld", "ld2", "gd", "outd"] + [f"L{i}" for i in range(8)] + [f"G{i}" for i in range(8)]:
                k.events["end_" + cn] = k.ctr[cn]
        k.op("gpsimd", lambda g: None,
             waits=[("gd", "end_gd"), ("outd", "end_outd")] +
                   [(f"G{i}", f"end_G{i}") for i in range(8)])
        k.op("sync", lambda e: None,
             waits=[("ld", "end_ld")] + [(f"L{i}", f"end_L{i}") for i in range(8)])
        k.op("scalar", lambda e: None, waits=[("ld2", "end_ld2")])

    def program(k: K):
        op = k.op
        ev = k.events
        wt_seq = [0]

        def ldma(out_ap, in_ap, waits=(), key=None):
            lane = f"L{k.lane_rr % 8}"
            k.lane_rr += 1
            op("sync", lambda e, out_ap=out_ap, in_ap=in_ap: e.dma_start(
                out=out_ap, in_=in_ap), waits=waits, inc=lane, inc_n=16, key=key)

        def gdma(out_ap, in_ap, waits=(), key=None):
            lane = f"G{k.glane_rr % 8}"
            k.glane_rr += 1
            op("gpsimd", lambda e, out_ap=out_ap, in_ap=in_ap: e.dma_start(
                out=out_ap, in_=in_ap), waits=waits, inc=lane, inc_n=16, key=key)

        def wt_load(src_ap, extra_waits=()):
            n = wt_seq[0]; wt_seq[0] += 1
            wi = n % 12
            w = list(extra_waits)
            if n >= 12:
                w.append(("t", ("wtfree", n - 12)))
            ldma(wt[wi], src_ap, waits=w, key=("wld", n))
            return n, wi

        op("gpsimd", lambda g: g.iota(kpos, pattern=[[128, NCH_B]], base=0,
                                      channel_multiplier=1, allow_small_or_imprecise_dtypes=True))
        op("gpsimd", lambda g: g.iota(mbig, pattern=[[-1, 1152]], base=512,
                                      channel_multiplier=1, allow_small_or_imprecise_dtypes=True))
        op("gpsimd", lambda g: g.iota(ident, pattern=[[-1, 128]], base=0,
                                      channel_multiplier=1, allow_small_or_imprecise_dtypes=True),
           inc="g", key="consts_g")
        def cfn(v):
            v.tensor_scalar(out=mbig, in0=mbig, scalar1=0.0, scalar2=NEG,
                            op0=ALU.is_gt, op1=ALU.mult)
            v.tensor_scalar(out=ident, in0=ident, scalar1=0.0, scalar2=1.0,
                            op0=ALU.is_equal, op1=ALU.mult)
            v.memset(ones1, 1.0)
            v.memset(ones_row, 1.0)
            v.drain()
        op("vector", cfn, waits=[("g", "consts_g")], inc="v", key="consts_v")

        op("sync", lambda e: e.dma_start(out=jm_sb[:, 0:NCH_A], in_=jma), inc="ld", inc_n=16)
        op("sync", lambda e: e.dma_start(out=jm_sb[:, NCH_A:], in_=jmb), inc="ld", inc_n=16)
        op("sync", lambda e: e.dma_start(out=fb_sb, in_=fbias), inc="ld", inc_n=16,
           key="smalls_ld")

        for r in range(4):
            w = [("v", ("castv", r - 1))] if r > 0 else []
            op("sync", lambda e, r=r: e.dma_start(out=cast_in, in_=wqkv_sh[r*128:(r+1)*128, :]),
               waits=w, inc="ld", inc_n=16)
            op("sync", lambda e, r=r: e.dma_start(out=cast_in2, in_=wo_sh[r*128:(r+1)*128, :]),
               inc="ld", inc_n=16, key=("castld", r))
            def cast_fn(v):
                v.tensor_copy(out=cast_out[:, 0:QC], in_=cast_in)
                v.tensor_copy(out=cast_out[:, QC:], in_=cast_in2)
                v.drain()
            w2 = [("ld", ("castld", r))]
            if r > 0:
                w2.append(("gd", ("castst", r - 1)))
            op("vector", cast_fn, waits=w2, inc="v", key=("castv", r))
            op("gpsimd", lambda g, r=r: g.dma_start(out=agw_in[r*128:(r+1)*128, :],
                                                    in_=cast_out),
               waits=[("v", ("castv", r))], inc="gd", inc_n=16, key=("castst", r))
        op("gpsimd", lambda g: g.collective_compute(
            "AllGather", ALU.bypass, replica_groups=[list(range(8))],
            ins=[agw_in], outs=[agw_out]),
           waits=[("gd", ("castst", 3))], inc="cc", key="ag1")

        for dc in range(32):
            op("scalar", lambda e, dc=dc: e.dma_start(out=wkv_sb[:, dc*256:dc*256+128],
                                                      in_=wq_tile(dc, D)),
               waits=([("cc", "ag1")] if dc == 0 else []), inc="ld2", inc_n=16)
            op("scalar", lambda e, dc=dc: e.dma_start(out=wkv_sb[:, dc*256+128:dc*256+256],
                                                      in_=wq_tile(dc, D + DH)),
               inc="ld2", inc_n=16, key=("wkvld", dc))

        def ln_group(tag, g, src_ap, extra_xld_waits=(), ln2_group=None):
            ring = len(k.hist["xg"]) % 2
            b = 8 * ring
            prev_lnv = k.use("xg", 2, (tag, "lnv", g))
            prev_tr = k.use("lnxr", 2, (tag, "trlast", g))
            w0 = list(extra_xld_waits)
            if prev_lnv is not None:
                w0.append(("v", prev_lnv))
            lane = f"L{k.lane_rr % 8}"
            k.lane_rr += 1
            op("sync", lambda e: e.dma_start(out=xg[ring], in_=src_ap),
               waits=w0, inc=lane, inc_n=16, key=(tag, "xld", g))
            def v1(v):
                v.reduce_sum(out=cols[:, b:b+1], in_=xg[ring], axis=AX.X)
                v.drain()
                v.tensor_scalar(out=cols[:, b+1:b+2], in0=cols[:, b:b+1],
                                scalar1=1.0/D, scalar2=0.0, op0=ALU.mult, op1=ALU.add)
                v.drain()
            op("vector", v1, waits=[("EV", (tag, "xld", g))], inc="v", key=(tag, "mean", g))
            wsq = [("v", (tag, "mean", g))]
            if prev_tr is not None:
                wsq.append(("t", prev_tr))
            op("scalar", lambda s: s.activation(out=lnxr[ring], in_=xg[ring], func=AF.Square,
                                                bias=cols[:, b+1:b+2], scale=-1.0,
                                                accum_out=cols[:, b+2:b+3]),
               waits=wsq, inc="s", key=(tag, "sq", g))
            def v2(v):
                v.tensor_scalar(out=cols[:, b+3:b+4], in0=cols[:, b+2:b+3],
                                scalar1=1.0/D, scalar2=1e-5, op0=ALU.mult, op1=ALU.add)
                v.drain()
                v.reciprocal(out=cols[:, b+4:b+5], in_=cols[:, b+3:b+4])
                v.drain()
            op("vector", v2, waits=[("s", (tag, "sq", g))], inc="v", key=(tag, "rv", g))
            op("scalar", lambda s: s.activation(out=cols[:, b+5:b+6], in_=cols[:, b+4:b+5],
                                                func=AF.Sqrt, bias=0.0, scale=1.0),
               waits=[("v", (tag, "rv", g))], inc="s", key=(tag, "rs", g))
            if ln2_group is not None:
                gg = ln2_group
                def v3(v):
                    v.tensor_tensor(out=cols[:, b+6:b+7], in0=cols[:, b+2:b+3],
                                    in1=cols[:, b+4:b+5], op=ALU.mult)
                    v.drain()
                op("vector", v3, waits=[("s", (tag, "rs", g))], inc="v", key=(tag, "e1", g))
                op("scalar", lambda s: s.activation(out=cols[:, b+6:b+7], in_=cols[:, b+6:b+7],
                                                    func=AF.Sqrt, bias=0.0, scale=1.0),
                   waits=[("v", (tag, "e1", g))], inc="s", key=(tag, "e2", g))
                def v4(v):
                    v.tensor_scalar(out=mcol[:, gg:gg+1], in0=cols[:, b+6:b+7],
                                    scalar1=64.0, scalar2=1.0, op0=ALU.is_gt, op1=ALU.mult)
                    v.drain()
                op("vector", v4, waits=[("s", (tag, "e2", g))], inc="v", key=(tag, "mask", g))
            def v5(v):
                v.tensor_scalar(out=lnxr[ring], in0=xg[ring],
                                scalar1=cols[:, b+1:b+2], scalar2=cols[:, b+5:b+6],
                                op0=ALU.subtract, op1=ALU.mult)
                v.drain()
            op("vector", v5, waits=[("s", (tag, "rs", g))], inc="v", key=(tag, "lnv", g))
            return ring

        def tr_group(tag, g, ring, dest, gl, tokw, extra_evac_waits=(),
                     extra_tr_waits=()):
            dap = dest.rearrange("p (c t) -> p c t", t=tokw)
            for dcq in range(8):
                bank = p_tr if dcq % 2 == 0 else p_tr2
                for i in range(4):
                    dc = dcq * 4 + i
                    w = []
                    if i == 0:
                        w.append(("v", (tag, "lnv", g)))
                        if dcq < 2:
                            w.extend(extra_tr_waits)
                        if dcq >= 2:
                            w.append(("s", (tag, "trev", g, dcq - 2)))
                    kk = (tag, "trlast", g) if (dcq == 7 and i == 3) else (tag, "tr", g, dcq, i)
                    op("tensor", lambda t, bank=bank, i=i, dc=dc, ring=ring: t.transpose(
                        bank[:, i*128:(i+1)*128], lnxr[ring][:, dc*128:(dc+1)*128], ident),
                       waits=w, inc="t", key=kk)
                dst = dap[:, dcq*4:dcq*4+4, gl*128:(gl+1)*128]
                srcv = bank[:].rearrange("p (c t) -> p c t", t=128)
                wk = (tag, "trlast", g) if dcq == 7 else (tag, "tr", g, dcq, 3)
                w2 = [("t", wk)] + list(extra_evac_waits)
                op("scalar", lambda s, dst=dst, srcv=srcv: s.copy(out=dst, in_=srcv),
                   waits=w2, inc="s", key=(tag, "trev", g, dcq))
            if k.counting:
                ev[(tag, "trdone", g)] = ev[(tag, "trev", g, 7)]

        if not live("P1"):
            return finish(k)
        for sname, xsrc, slen, ktdst, vdst, tix in (
                ("A", xa, SLAB_A, kta, va, 0), ("B", xb, SLAB_B, ktb, vb, 1)):
            tag = ("P1", sname)
            nsg = slen // 512
            for sg in range(nsg):
                own = (sg == nsg - 1)
                dest = qsrc[tix] if own else lnt
                for gl in range(4):
                    g = sg * 4 + gl
                    exw = []
                    if sname == "A" and g < 2:
                        exw.append(("gd", ("castst", 3)))
                    ring = ln_group(tag, g, xsrc[g*128:(g+1)*128, :], extra_xld_waits=exw)
                    eev = []
                    if not own and sg > 0:
                        eev.append(("t", (tag, "vmm", sg - 1, 3, 31)))
                    if own:
                        eev.append(("gd", ("castst", 3)))
                    tr_group(tag, g, ring, dest, gl, 512, extra_evac_waits=eev)
                src = dest
                for dc in range(32):
                    w = []
                    if dc == 0:
                        w = [("s", (tag, "trdone", sg * 4 + 3)), ("ld2", ("wkvld", 31)),
                             ("s", (tag, "ktev", sg - 1))]
                        if sname == "B":
                            w.append(("s", (("P1", "A"), "ktev", SLAB_A // 512 - 1)))
                    op("tensor", lambda t, dc=dc, src=src: t.matmul(
                        p_mm[0][:], wkv_sb[:, dc*256:dc*256+128], src[:, dc*512:(dc+1)*512],
                        start=(dc == 0), stop=(dc == 31)),
                       waits=w, inc="t", key=(tag, "ktmm", sg, dc))
                op("scalar", lambda s, sg=sg, ktdst=ktdst: s.copy(
                    out=ktdst[:, sg*512:(sg+1)*512], in_=p_mm[0][:]),
                   waits=[("t", (tag, "ktmm", sg, 31))], inc="s", key=(tag, "ktev", sg))
                for tg in range(4):
                    for dc in range(32):
                        w = []
                        if dc == 0:
                            w = [("s", (tag, "vev", sg, tg - 1)),
                                 ("s", (tag, "vev", sg - 1, 3))]
                            if sname == "B" and sg == 0 and tg == 0:
                                w.append(("s", (("P1", "A"), "vev", SLAB_A // 512 - 1, 3)))
                        op("tensor", lambda t, dc=dc, tg=tg, src=src: t.matmul(
                            p_mm[1][:, 0:128], src[:, dc*512+tg*128:dc*512+tg*128+128],
                            wkv_sb[:, dc*256+128:dc*256+256],
                            start=(dc == 0), stop=(dc == 31)),
                           waits=w, inc="t", key=(tag, "vmm", sg, tg, dc))
                    op("scalar", lambda s, sg=sg, tg=tg, vdst=vdst: s.copy(
                        out=vdst[:, (sg*4+tg)*128:(sg*4+tg)*128+128], in_=p_mm[1][:, 0:128]),
                       waits=[("t", (tag, "vmm", sg, tg, 31))], inc="s",
                       key=(tag, "vev", sg, tg))

        if "lnxb" in dbg:
            op("gpsimd", lambda g: g.dma_start(out=dbg["lnxb"], in_=lnxr[1]),
               waits=[("v", (("P1", "B"), "lnv", 31))], inc="gd", inc_n=16)
        if "ktb" in dbg:
            op("gpsimd", lambda g: g.dma_start(out=dbg["ktb"], in_=ktb),
               waits=[("s", (("P1", "B"), "ktev", 7))], inc="gd", inc_n=16)
        if "vb" in dbg:
            op("gpsimd", lambda g: g.dma_start(out=dbg["vb"], in_=vb),
               waits=[("s", (("P1", "B"), "vev", 7, 3))], inc="gd", inc_n=16)

        if not live("P23"):
            return finish(k)
        for tix, (sname, slen, nch, ktsrc, vsrc) in enumerate(
                (("A", SLAB_A, NCH_A, kta, va), ("B", SLAB_B, NCH_B, ktb, vb))):
            ptag = ("P2", sname)
            atag = ("P3", sname)
            nsg = slen // 512
            slab = slen
            bias_i = 0
            prev_nc = [0]
            last_mm1_of_head = {}
            for h in range(NH):
                loads = []
                for dc in range(32):
                    ew = []
                    if h == 0 and dc == 0:
                        ew.append(("cc", "ag1"))
                        if sname == "B":
                            ew.append(("v", (("P3", "A"), "done")))
                    loads.append(wt_load(wq_tile(dc, h * 128), extra_waits=ew))
                pm = p_mm[h % 2]
                for dc in range(32):
                    n, wi = loads[dc]
                    w = [("EV", ("wld", n))]
                    if dc == 0:
                        w.append(("s", (ptag, "qev", h - 2)))
                        w.append(("s", (("P1", sname), "trdone", slen // 128 - 1)))
                    op("tensor", lambda t, pm=pm, wi=wi, dc=dc, tix=tix: t.matmul(
                        pm[:], wt[wi], qsrc[tix][:, dc*512:(dc+1)*512],
                        start=(dc == 0), stop=(dc == 31)),
                       waits=w, inc="t", key=("wtfree", n))
                    if k.counting:
                        ev[(ptag, "qmm", h, dc)] = ev[("wtfree", n)]
                wq_ev = [("t", (ptag, "qmm", h, 31))]
                if h >= 2 and not no_attn:
                    wq_ev.append(("t", last_mm1_of_head[h - 2]))
                if "qtall" in dbg and h >= 2:
                    wq_ev.append(("gd", (ptag, "qtd", h - 2)))
                op("scalar", lambda s, h=h, pm=pm: s.activation(
                    out=qt_sm[h % 2], in_=pm[:], func=AF.Copy, bias=0.0, scale=SCALE),
                   waits=wq_ev, inc="s", key=(ptag, "qev", h))
                if "qtall" in dbg and sname == "B":
                    op("gpsimd", lambda g, h=h: g.dma_start(
                        out=dbg["qtall"][h], in_=qt_sm[h % 2]),
                       waits=[("s", (ptag, "qev", h))], inc="gd", inc_n=16,
                       key=(ptag, "qtd", h))
                elif k.counting and "qtall" in dbg:
                    k.ctr["gd"] += 16
                    ev[(ptag, "qtd", h)] = k.ctr["gd"]
                elif "qtall" in dbg:
                    op("gpsimd", lambda g, h=h: g.dma_start(
                        out=dbg["qtall"][h], in_=qt_sm[h % 2]),
                       waits=[("s", (ptag, "qev", h))], inc="gd", inc_n=16,
                       key=(ptag, "qtd", h))
                if no_attn:
                    continue
                ns = nsub_for_head(h)
                wq = 512 // ns
                slope = float(SLOPES[h])
                for sub in range(ns):
                    bi = bias_i % 2
                    chunks = [ch for ch in range(nch)
                              if ch * 128 <= slab - 512 + (sub + 1) * wq - 1
                              and slope * ((slab - 512 + sub * wq)
                                           - (ch * 128 + 127)) < 50.0]
                    nc_ = len(chunks)
                    prev_mm3_last = (atag, "mm3", bias_i - 1, prev_nc[0] - 1) \
                        if bias_i > 0 else None
                    prev_nc[0] = nc_
                    Cs = slab - 512 + sub * wq + wq / 2.0 + 16.0
                    joff = 0 if sname == "A" else NCH_A
                    def biasfn(v, slope=slope, Cs=Cs, bi=bi, nch=nch, joff=joff):
                        v.tensor_scalar(out=biasc[bi][:, 0:nch], in0=kpos[:, 0:nch],
                                        scalar1=slope, scalar2=-slope * Cs,
                                        op0=ALU.mult, op1=ALU.add)
                        v.drain()
                        v.tensor_tensor(out=biasc[bi][:, 0:nch], in0=biasc[bi][:, 0:nch],
                                        in1=jm_sb[:, joff:joff+nch], op=ALU.add)
                        v.drain()
                    wb = [("s", (atag, "explast", bias_i - 2)), ("v", "consts_v"),
                          ("ld", "smalls_ld")]
                    op("vector", biasfn, waits=wb, inc="v", key=(atag, "bias", bias_i))
                    for ci, ch in enumerate(chunks):
                        cs = ch * 128
                        pb = p_s[ci % 2]
                        w = [("s", (atag, "exp", bias_i, ci - 1)), ("s", (ptag, "qev", h))]
                        if ci == 1:
                            w.append(("s", (atag, "explast", bias_i - 1)))
                        if ci == 0:
                            w.append(("v", (atag, "pvev", bias_i - 1)))
                            if bias_i >= 1:
                                w.append(("s", (atag, "bcev", bias_i - 1)))

                            if bias_i == 0:
                                w.append(("s", (("P1", sname), "ktev", nsg - 1)))
                                w.append(("s", (("P1", sname), "vev", nsg - 1, 3)))

                        op("tensor", lambda t, pb=pb, cs=cs, h=h, sub=sub, wq=wq,
                           ktsrc=ktsrc: t.matmul(
                            pb[:, 0:wq], ktsrc[:, cs:cs+128],
                            qt_sm[h % 2][:, sub*wq: sub*wq+wq],
                            start=True, stop=True),
                           waits=w, inc="t", key=(atag, "mm1", bias_i, ci))
                        last_mm1_of_head[h] = (atag, "mm1", bias_i, ci)
                        diag = cs >= slab - 512
                        if diag:
                            off = slab + sub * wq - cs
                            def mfn(v, pb=pb, off=off, wq=wq):
                                v.tensor_tensor(out=pb[:, 0:wq], in0=pb[:, 0:wq],
                                                in1=mbig[:, off:off+wq], op=ALU.add)
                                v.drain()
                            op("vector", mfn, waits=[("t", (atag, "mm1", bias_i, ci))],
                               inc="v", key=(atag, "madd", bias_i, ci))
                        we = [("t", (atag, "mm1", bias_i, ci)),
                              ("t", (atag, "mm3", bias_i, ci - 1)),
                              ("v", (atag, "bias", bias_i))]
                        if ci < 2 and prev_mm3_last is not None:
                            we.append(("t", prev_mm3_last))
                        if diag:
                            we.append(("v", (atag, "madd", bias_i, ci)))
                        op("scalar", lambda s, pb=pb, ch=ch, bi=bi, wq=wq, ci=ci: s.activation(
                            out=probs[ci % 2][:, 0:wq], in_=pb[:, 0:wq], func=AF.Exp,
                            bias=biasc[bi][:, ch:ch+1], scale=1.0),
                           waits=we, inc="s", key=(atag, "exp", bias_i, ci))
                        w2 = [("s", (atag, "exp", bias_i, ci))]
                        if ci == 0:
                            w2.append(("v", (atag, "pvev", bias_i - 1)))
                        op("tensor", lambda t, ch=ch, ci=ci, wq=wq, vsrc=vsrc,
                           first=(ci == 0), last=(ci == nc_ - 1): t.matmul(
                            p_pv[:, 0:wq], vsrc[:, ch*128:(ch+1)*128],
                            probs[ci % 2][:, 0:wq], start=first, stop=last),
                           waits=w2, inc="t", key=(atag, "mm2", bias_i, ci))
                        op("tensor", lambda t, ci=ci, wq=wq,
                           first=(ci == 0), last=(ci == nc_ - 1): t.matmul(
                            p_den[0:1, 0:wq], ones1, probs[ci % 2][:, 0:wq],
                            start=first, stop=last),
                           inc="t", key=(atag, "mm3", bias_i, ci))
                    if k.counting:
                        ev[(atag, "explast", bias_i)] = ev[(atag, "exp", bias_i, nc_ - 1)]
                    def rfn(v, wq=wq):
                        v.reciprocal(out=rden[:, 0:wq], in_=p_den[0:1, 0:wq])
                        v.drain()
                        v.tensor_copy(out=rden_hi[:, 0:wq], in_=rden[:, 0:wq])
                        v.drain()
                        v.tensor_tensor(out=rden_lo[:, 0:wq], in0=rden[:, 0:wq],
                                        in1=rden_hi[:, 0:wq], op=ALU.subtract)
                        v.drain()
                    op("vector", rfn, waits=[("t", (atag, "mm3", bias_i, nc_ - 1)),
                                             ("t", (atag, "bc", bias_i - 1))],
                       inc="v", key=(atag, "recip", bias_i))
                    op("tensor", lambda t, wq=wq: t.matmul(
                        p_s[0][:, 0:wq], ones_row, rden_hi[:, 0:wq],
                        start=True, stop=False),
                       waits=[("v", (atag, "recip", bias_i)),
                              ("s", (atag, "explast", bias_i))],
                       inc="t")
                    op("tensor", lambda t, wq=wq: t.matmul(
                        p_s[0][:, 0:wq], ones_row, rden_lo[:, 0:wq],
                        start=False, stop=True),
                       inc="t", key=(atag, "bc", bias_i))
                    op("scalar", lambda s, wq=wq: s.copy(out=bcast[:, 0:wq],
                                                         in_=p_s[0][:, 0:wq]),
                       waits=[("t", (atag, "bc", bias_i)),
                              ("v", (atag, "pvev", bias_i - 1))], inc="s",
                       key=(atag, "bcev", bias_i))
                    def dfn(v, h=h, sub=sub, wq=wq, tix=tix):
                        v.tensor_tensor(out=attnv[tix][:, h*512+sub*wq: h*512+sub*wq+wq],
                                        in0=p_pv[:, 0:wq], in1=bcast[:, 0:wq], op=ALU.mult)
                        v.drain()
                    op("vector", dfn,
                       waits=[("s", (atag, "bcev", bias_i)),
                              ("t", (atag, "mm2", bias_i, nc_ - 1))],
                       inc="v", key=(atag, "pvev", bias_i))
                    bias_i += 1
            if no_attn:
                op("vector", lambda v: None, waits=[("s", (ptag, "qev", NH - 1))],
                   inc="v", key=(atag, "done2"))
            if k.counting:
                ev[(atag, "done")] = ev[(atag, "pvev", bias_i - 1)] if not no_attn \
                    else ev[(atag, "done2")]
            if "qt" in dbg and sname == "B":
                op("gpsimd", lambda g: g.dma_start(out=dbg["qt"], in_=qt_sm[1]),
                   waits=[("v", (atag, "done"))], inc="gd", inc_n=16)
            if "qt0" in dbg and sname == "B":
                op("gpsimd", lambda g: g.dma_start(out=dbg["qt0"], in_=qt_sm[0]),
                   waits=[("v", (atag, "done"))], inc="gd", inc_n=16)
            if "probs" in dbg and sname == "B":
                op("gpsimd", lambda g: g.dma_start(out=dbg["probs"], in_=probs[1]),
                   waits=[("v", (atag, "done"))], inc="gd", inc_n=16)
            if "scores" in dbg and sname == "B":
                op("gpsimd", lambda g: g.dma_start(out=dbg["scores"], in_=p_s[1][:]),
                   waits=[("v", (atag, "done"))], inc="gd", inc_n=16)
            if "den" in dbg and sname == "B":
                op("gpsimd", lambda g: g.dma_start(out=dbg["den"], in_=bcast),
                   waits=[("v", (atag, "done"))], inc="gd", inc_n=16)
            if "att" in dbg and sname == "B":
                op("gpsimd", lambda g, tix=tix: g.dma_start(out=dbg["att"], in_=attnv[tix]),
                   waits=[("v", (atag, "done"))], inc="gd", inc_n=16)

        if not live("P4"):
            return finish(k)
        t4 = "P4"
        for oc in range(32):
            loads = []
            for dc in range(32):
                ew = []
                if oc == 0 and dc == 0:
                    ew += [("v", (("P3", "A"), "done")), ("v", (("P3", "B"), "done"))]
                loads.append(wt_load(wo_tile(dc, oc), extra_waits=ew))
            for dc in range(32):
                n, wi = loads[dc]
                w = [("EV", ("wld", n))]
                if dc == 0:
                    w.append(("s", (t4, "ev", oc - 1, 1)))
                op("tensor", lambda t, wi=wi, dc=dc: t.matmul(
                    p_mm[0][:], wt[wi], attnv[0][:, dc*512:(dc+1)*512],
                    start=(dc == 0), stop=(dc == 31)),
                   waits=w, inc="t")
                op("tensor", lambda t, wi=wi, dc=dc: t.matmul(
                    p_mm[1][:], wt[wi], attnv[1][:, dc*512:(dc+1)*512],
                    start=(dc == 0), stop=(dc == 31)),
                   inc="t", key=("wtfree", n))
                if k.counting:
                    ev[(t4, "mm", oc, dc)] = ev[("wtfree", n)]
            for half in range(2):
                op("scalar", lambda s, oc=oc, half=half: s.copy(
                    out=obig[:, oc*1024+half*512: oc*1024+half*512+512], in_=p_mm[half][:]),
                   waits=[("t", (t4, "mm", oc, 31))], inc="s", key=(t4, "ev", oc, half))
        for tokg in range(8):
            if tokg < 4:
                xsrc = xa[1536 + tokg*128: 1536 + tokg*128 + 128, :]
            else:
                xsrc = xb[3584 + (tokg-4)*128: 3584 + (tokg-4)*128 + 128, :]
            ldma(xr, xsrc, waits=[("v", (t4, "addv", tokg - 1, 7)),
                                  ("s", (t4, "ev", 31, 1))], key=(t4, "xrld", tokg))
            for ocg in range(8):
                bank = p_tr if ocg % 2 == 0 else p_tr2
                for i in range(4):
                    oc = ocg * 4 + i
                    w = []
                    if i == 0:
                        w = [("s", (t4, "ev", 31, 1)),
                             ("v", (t4, "addv", tokg, ocg - 2)),
                             ("v", (t4, "addv", tokg - 1, 7))]
                    op("tensor", lambda t, bank=bank, i=i, oc=oc, tokg=tokg: t.transpose(
                        bank[:, i*128:(i+1)*128],
                        obig[:, oc*1024+tokg*128: oc*1024+tokg*128+128], ident),
                       waits=w, inc="t", key=(t4, "tr", tokg, ocg, i))
                def addfn(v, bank=bank, ocg=ocg):
                    v.tensor_tensor(out=ostage[ocg % 2], in0=bank[:],
                                    in1=xr[:, ocg*512:(ocg+1)*512], op=ALU.add)
                    v.drain()
                op("vector", addfn,
                   waits=[("t", (t4, "tr", tokg, ocg, 3)), ("EV", (t4, "xrld", tokg)),
                          ("EV", (t4, "xst", tokg, ocg - 2)),
                          ("EV", (t4, "xst", tokg - 1, 7))],
                   inc="v", key=(t4, "addv", tokg, ocg))
                gdma(xmid_d[tokg*128:(tokg+1)*128, ocg*512:(ocg+1)*512],
                     ostage[ocg % 2],
                     waits=[("v", (t4, "addv", tokg, ocg))],
                     key=(t4, "xst", tokg, ocg))
        if k.counting:
            ev[(t4, "xmid_done")] = ev[(t4, "xst", 7, 7)]
            k.evsem[(t4, "xmid_done")] = k.evsem[(t4, "xst", 7, 7)]
            ev[(t4, "tok_done")] = ev[(t4, "addv", 7, 7)]
        if "xmid" in dbg:
            op("gpsimd", lambda g: g.dma_start(out=dbg["xmid"], in_=xmid_d),
               waits=[("EV", (t4, "xmid_done"))], inc="gd", inc_n=16)

        if not live("DQ"):
            return finish(k)
        def dequant(planes, lutsrc, lutbuf, nclass, agdst, tag):
            npl = len(planes)
            for ot in range(4):
                for pi, p in enumerate(planes):
                    w = [("v", (tag, "idxf", ot - 1))] if pi == 0 else []
                    if ot == 0 and pi == 0:
                        w.append(("s", (t4, "ev", 31, 1)))
                        w.append(("v", (t4, "tok_done")))
                    ldma(dq_q[:, pi*128:(pi+1)*128], qw_sh[p, ot*128:(ot+1)*128, :],
                         waits=w)
                ldma(lutbuf[:, 0:nclass], lutsrc[ot*128:(ot+1)*128, :],
                     waits=[("v", (tag, "prod", ot - 1, 3))],
                     key=(tag, "dqld", ot))
                def unp(v, npl=npl):
                    for pi in range(npl):
                        for bbit in range(32):
                            shf = bbit - pi
                            dst = dq_acc[:, pi*D + bbit: pi*D + bbit + 127*32 + 1: 32]
                            if shf >= 0:
                                v.tensor_scalar(out=dst, in0=dq_q[:, pi*128:(pi+1)*128],
                                                scalar1=shf, scalar2=(1 << pi),
                                                op0=ALU.logical_shift_right, op1=ALU.bitwise_and)
                            else:
                                v.tensor_scalar(out=dst, in0=dq_q[:, pi*128:(pi+1)*128],
                                                scalar1=-shf, scalar2=(1 << pi),
                                                op0=ALU.logical_shift_left, op1=ALU.bitwise_and)
                    v.drain()
                    v.tensor_tensor(out=dq_acc[:, 0:D], in0=dq_acc[:, 0:D],
                                    in1=dq_acc[:, D:2*D], op=ALU.bitwise_or)
                    v.tensor_tensor(out=dq_acc[:, 2*D:3*D], in0=dq_acc[:, 2*D:3*D],
                                    in1=dq_acc[:, 3*D:4*D], op=ALU.bitwise_or)
                    v.drain()
                    v.tensor_tensor(out=dq_acc[:, 0:D], in0=dq_acc[:, 0:D],
                                    in1=dq_acc[:, 2*D:3*D], op=ALU.bitwise_or)
                    v.drain()
                    v.tensor_copy(out=dq_idxf, in_=dq_acc[:, 0:D])
                    v.drain()
                wunp = [("EV", (tag, "dqld", ot)), ("t", (tag, "trp", ot - 1, 7))]
                if ot == 0:
                    wunp.append(("v", (t4, "tok_done")))
                op("vector", unp, waits=wunp, inc="v", key=(tag, "idxf", ot))
                for icq in range(4):
                    def prod(v, icq=icq, nclass=nclass, lutbuf=lutbuf):
                        for c in range(nclass):
                            v.tensor_scalar(out=dq_prod[:, c*1024:(c+1)*1024],
                                            in0=dq_idxf[:, icq*1024:(icq+1)*1024],
                                            scalar1=float(c), scalar2=lutbuf[:, c:c+1],
                                            op0=ALU.is_equal, op1=ALU.mult)
                        v.drain()
                        step = 1
                        while step < nclass:
                            for c in range(0, nclass, 2 * step):
                                v.tensor_tensor(out=dq_prod[:, c*1024:(c+1)*1024],
                                                in0=dq_prod[:, c*1024:(c+1)*1024],
                                                in1=dq_prod[:, (c+step)*1024:(c+step+1)*1024],
                                                op=ALU.add)
                            v.drain()
                            step *= 2
                        v.tensor_copy(out=dq_w[:, icq*1024:(icq+1)*1024],
                                      in_=dq_prod[:, 0:1024])
                        v.drain()
                    op("vector", prod, waits=[("v", (tag, "idxf", ot))],
                       inc="v", key=(tag, "prod", ot, icq))
                for icq in range(8):
                    bank = p_tr if icq % 2 == 0 else p_tr2
                    for i in range(4):
                        ic = icq * 4 + i
                        w = []
                        if i == 0:
                            w.append(("v", (tag, "prod", ot, ic // 8)))
                            if icq >= 2:
                                w.append(("s", (tag, "trev", ot, icq - 2)))
                        op("tensor", lambda t, bank=bank, i=i, ic=ic: t.transpose(
                            bank[:, i*128:(i+1)*128], dq_w[:, ic*128:(ic+1)*128], ident),
                           waits=w, inc="t",
                           key=((tag, "trp", ot, icq) if i == 3 else None))
                    op("scalar", lambda s, bank=bank, icq=icq: s.copy(
                        out=dq_pack[icq % 2], in_=bank[:]),
                       waits=[("t", (tag, "trp", ot, icq)),
                              ("EV", (tag, "wst", ot, icq - 2))],
                       inc="s", key=(tag, "trev", ot, icq))
                    gdma(agdst[icq*4:(icq+1)*4, ot]
                         .rearrange("c p o -> p c o"),
                         dq_pack[icq % 2].rearrange("p (c o) -> p c o", o=128),
                         waits=[("s", (tag, "trev", ot, icq))],
                         key=(tag, "wst", ot, icq))
            if k.counting:
                ev[(tag, "done")] = ev[(tag, "wst", 3, 7)]
                ev[(tag, "vdone")] = ev[(tag, "prod", 3, 3)]

        dequant([0, 1, 2, 3], lut4_sh, dq_lut, 16, agl_in, ("DQ", "l"))
        if "dqw" in dbg:
            op("gpsimd", lambda g: g.dma_start(out=dbg["dqw"], in_=dq_w),
               waits=[("v", (("DQ", "l"), "prod", 3, 3)),
                      ("t", (("DQ", "l"), "trp", 3, 7))], inc="gd", inc_n=16)
        if "idxf" in dbg:
            op("gpsimd", lambda g: g.dma_start(out=dbg["idxf"], in_=dq_idxf),
               waits=[("v", (("DQ", "l"), "prod", 3, 3))], inc="gd", inc_n=16)
        if k.counting:
            for i in range(8):
                ev[("ag2g", i)] = k.ctr[f"G{i}"]
        op("gpsimd", lambda g: g.collective_compute(
            "AllGather", ALU.bypass, replica_groups=[list(range(8))],
            ins=[agl_in], outs=[agl_out]),
           waits=[(f"G{i}", ("ag2g", i)) for i in range(8)], inc="cc", key="ag2")
        if "wl" in dbg:
            op("gpsimd", lambda g: g.dma_start(out=dbg["wl"], in_=agl_in),
               waits=[(f"G{i}", ("ag2g", i)) for i in range(8)], inc="gd", inc_n=16)

        if not live("P5"):
            return finish(k)
        t5 = "P5"
        for g in range(8):
            exw = [("EV", (t4, "xst", g, 7))]
            if g < 2:
                exw.append(("v", (("DQ", "l"), "idxf", 3)))
            ring = ln_group(t5, g, xmid_d[g*128:(g+1)*128, :],
                            extra_xld_waits=exw, ln2_group=g)
            eev = []
            etw = []
            if g == 0:
                eev = [("t", (("DQ", "l"), "trp", 3, 7))]
                etw = [("s", (("DQ", "l"), "trev", 3, 7)),
                       ("s", (("DQ", "l"), "trev", 3, 6))]
            tr_group(t5, g, ring, ln2t, g, 1024, extra_evac_waits=eev,
                     extra_tr_waits=etw)
        if "mcol" in dbg:
            op("gpsimd", lambda gg: gg.dma_start(out=dbg["mcol"], in_=mcol),
               waits=[("v", (t5, "mask", 7))], inc="gd", inc_n=16)

        if not live("P6"):
            return finish(k)
        t6 = "P6"
        for oc in range(32):
            gsh, csh = oc // 4, (oc % 4) * 128
            pmb = p_mm if oc % 2 == 0 else p_s
            loads = []
            for ic in range(32):
                ew = []
                if oc == 0 and ic == 0:
                    ew += [("cc", "ag2"), ("s", (t5, "trdone", 7))]
                loads.append(wt_load(agl_out[gsh, ic, csh // 128],
                                     extra_waits=ew))
            for ic in range(32):
                n, wi = loads[ic]
                w = [("EV", ("wld", n))]
                if ic == 0:
                    w.append(("EV", (t6, "ev", oc - 2, 1)))
                op("tensor", lambda t, pmb=pmb, wi=wi, ic=ic: t.matmul(
                    pmb[0][:], wt[wi], ln2t[:, ic*1024: ic*1024+512],
                    start=(ic == 0), stop=(ic == 31)),
                   waits=w, inc="t")
                op("tensor", lambda t, pmb=pmb, wi=wi, ic=ic: t.matmul(
                    pmb[1][:], wt[wi], ln2t[:, ic*1024+512: ic*1024+1024],
                    start=(ic == 0), stop=(ic == 31)),
                   inc="t", key=("wtfree", n))
                if k.counting:
                    ev[(t6, "mm", oc, ic)] = ev[("wtfree", n)]
            for half in range(2):
                def fev(v, oc=oc, half=half, pmb=pmb):
                    v.tensor_scalar(out=obig[:, oc*1024+half*512: oc*1024+half*512+512],
                                    in0=pmb[half][:], scalar1=fb_sb[:, oc:oc+1],
                                    scalar2=0.0, op0=ALU.add, op1=ALU.add)
                    v.drain()
                op("vector", fev, waits=[("t", (t6, "mm", oc, 31))],
                   inc="v", key=(t6, "ev", oc, half))
        for tokg in range(8):
            ldma(xr, xmid_d[tokg*128:(tokg+1)*128, :],
                 waits=[("v", (t6, "addv", tokg - 1, 7)), ("EV", (t6, "ev", 31, 1)),
                        ("EV", (t4, "xmid_done"))],
                 key=(t6, "xrld", tokg))
            for ocg in range(8):
                bank = p_tr if ocg % 2 == 0 else p_tr2
                for i in range(4):
                    oc = ocg * 4 + i
                    w = []
                    if i == 0:
                        w = [("EV", (t6, "ev", 31, 1)),
                             ("v", (t6, "addv", tokg, ocg - 2)),
                             ("v", (t6, "addv", tokg - 1, 7))]
                    op("tensor", lambda t, bank=bank, i=i, oc=oc, tokg=tokg: t.transpose(
                        bank[:, i*128:(i+1)*128],
                        obig[:, oc*1024+tokg*128: oc*1024+tokg*128+128], ident),
                       waits=w, inc="t", key=(t6, "tr", tokg, ocg, i))
                def addfn(v, bank=bank, ocg=ocg):
                    v.tensor_tensor(out=ostage[ocg % 2], in0=bank[:],
                                    in1=xr[:, ocg*512:(ocg+1)*512], op=ALU.add)
                    v.drain()
                op("vector", addfn,
                   waits=[("t", (t6, "tr", tokg, ocg, 3)), ("EV", (t6, "xrld", tokg)),
                          ("EV", (t6, "ost", tokg, ocg - 2)),
                          ("EV", (t6, "ost", tokg - 1, 7))],
                   inc="v", key=(t6, "addv", tokg, ocg))
                gdma(out[tokg*128:(tokg+1)*128, ocg*512:(ocg+1)*512],
                     ostage[ocg % 2],
                     waits=[("v", (t6, "addv", tokg, ocg))],
                     key=(t6, "ost", tokg, ocg))

        ty = "YH"
        if force_mask:
            op("vector", lambda v: (v.memset(mcol, 1.0), v.drain(), None)[2],
               waits=[("v", (t5, "mask", 7))], inc="v", key="fmask")
        op("vector", lambda v: (v.tensor_copy(out=mcb, in_=mcol), v.drain(), None)[2],
           waits=[("v", (t5, "mask", 7))] + ([("v", "fmask")] if force_mask else []),
           inc="v", key="mcb")
        op("tensor", lambda t: t.matmul(p_den[0:1, 0:8], ones1, mcb,
                                        start=True, stop=True),
           waits=[("v", "mcb"), ("v", (("P3", "B"), "done"))], inc="t", key="fmm")
        def ffl(v):
            v.reduce_sum(out=flagv[:, 0:1], in_=p_den[0:1, 0:8], axis=AX.X)
            v.drain()
            v.tensor_scalar(out=flagv[:, 1:2], in0=flagv[:, 0:1], scalar1=0.0,
                            scalar2=1.0, op0=ALU.is_gt, op1=ALU.mult)
            v.drain()
        op("vector", ffl, waits=[("t", "fmm")], inc="v", key="fv1")
        op("gpsimd", lambda g: g.dma_start(out=agf_in, in_=flagv[0:1, 0:2]),
           waits=[("v", "fv1")], inc="gd", inc_n=16, key="fst")
        op("gpsimd", lambda g: g.collective_compute(
            "AllReduce", ALU.max, replica_groups=[list(range(8))],
            ins=[agf_in], outs=[agf_out]),
           waits=[("gd", "fst")], inc="cc", key="frd")
        op("gpsimd", lambda g: g.dma_start(out=flagv[0:1, 2:4], in_=agf_out),
           waits=[("cc", "frd")], inc="gd", inc_n=16, key="fld")
        def ffi(v):
            v.tensor_scalar(out=flagi[:, 0:1], in0=flagv[:, 3:4], scalar1=0.0,
                            scalar2=1.0, op0=ALU.is_gt, op1=ALU.mult)
            v.drain()
            if force_slow:
                v.memset(flagi[:, 0:1], 1)
                v.drain()
        op("vector", ffi, waits=[("gd", "fld")], inc="v", key="flagi")
        op("gpsimd", lambda g: g.dma_start(out=flagi_d, in_=flagi[0:1, 0:2]),
           waits=[("v", "flagi")], inc="gd", inc_n=16, key="fst2")
        ldma(flagi2[0:1, 0:2], flagi_d, waits=[("gd", "fst2")], key="fli")
        if "flag" in dbg:
            def fdump(v):
                v.tensor_copy(out=flagv[:, 2:3], in_=flagv[:, 2:3])
            op("gpsimd", lambda g: g.dma_start(out=dbg["flag"][0:1, 0:4], in_=flagv[0:1, 0:4]),
               waits=[("v", "flagi")], inc="gd", inc_n=16)
            op("gpsimd", lambda g: g.dma_start(
                out=dbg["flag"][0:1, 4:6], in_=flagi[0:1, 0:2].bitcast(F32)),
               inc="gd", inc_n=16)

        ylane = [0]
        def ydma(out_ap, in_ap, waits=(), key=None):
            lane = f"Y{ylane[0] % 4}"
            ylane[0] += 1
            op("sync", lambda e, out_ap=out_ap, in_ap=in_ap: e.dma_start(
                out=out_ap, in_=in_ap), waits=waits, inc=lane, inc_n=16, key=key)
        yglane = [0]
        def ygdma(out_ap, in_ap, waits=(), key=None):
            lane = f"YG{yglane[0] % 4}"
            yglane[0] += 1
            op("gpsimd", lambda e, out_ap=out_ap, in_ap=in_ap: e.dma_start(
                out=out_ap, in_=in_ap), waits=waits, inc=lane, inc_n=16, key=key)

        def slow_section():
            if k.counting:
                for i in range(8):
                    ev[("yhg", i)] = k.ctr[f"G{i}"]
            if yh_level < 2:
                op("vector", lambda v: (v.memset(y_tmp, 0.0), v.drain(), None)[2],
                   inc="yv", key=(ty, "triv"))
                return
            def dequant_loop():
              for ot in range(4):
                  w0 = [(f"G{i}", ("yhg", i)) for i in range(8)] if ot == 0 else \
                       ([("yv", (ty, "red", ot - 1, 7))] if yh_level >= 22
                        else [("yv", (ty, "idx", ot - 1, 3))])
                  for p in range(8):
                      ydma(y_q8[:, p*128:(p+1)*128], qw_sh[p, ot*128:(ot+1)*128, :],
                           waits=(w0 if p == 0 else ()), key=(ty, "qld", ot, p))
                  ydma(y_lut8, lut8_sh[ot*128:(ot+1)*128, :], waits=w0, key=(ty, "ld", ot))
                  for ic4 in range(4):
                      def unp8(v, ic4=ic4):
                          for p in range(8):
                              for b in range(32):
                                  dst = y_acc[:, p*1024 + b: p*1024 + b + 31*32 + 1: 32]
                                  shf = b - p
                                  if shf >= 0:
                                      v.tensor_scalar(out=dst,
                                                      in0=y_q8[:, p*128+ic4*32: p*128+ic4*32+32],
                                                      scalar1=shf, scalar2=(1 << p),
                                                      op0=ALU.logical_shift_right,
                                                      op1=ALU.bitwise_and)
                                  else:
                                      v.tensor_scalar(out=dst,
                                                      in0=y_q8[:, p*128+ic4*32: p*128+ic4*32+32],
                                                      scalar1=-shf, scalar2=(1 << p),
                                                      op0=ALU.logical_shift_left,
                                                      op1=ALU.bitwise_and)
                          v.drain()
                          for st in (1, 2, 4):
                              for c in range(0, 8, 2 * st):
                                  v.tensor_tensor(out=y_acc[:, c*1024:(c+1)*1024],
                                                  in0=y_acc[:, c*1024:(c+1)*1024],
                                                  in1=y_acc[:, (c+st)*1024:(c+st+1)*1024],
                                                  op=ALU.bitwise_or)
                              v.drain()
                          v.tensor_copy(out=y_idxf[:, ic4*1024:(ic4+1)*1024],
                                        in_=y_acc[:, 0:1024])
                          v.drain()
                      wu = [("EV", (ty, "ld", ot))] + \
                           [("EV", (ty, "qld", ot, p)) for p in range(8)]
                      if ot > 0 and ic4 == 0 and yh_level >= 22:
                          wu.append(("yv", (ty, "red", ot - 1, 7)))
                      op("vector", unp8, waits=wu, inc="yv", key=(ty, "idx", ot, ic4))
                  if yh_level < 22:
                      continue
                  for ic4 in range(4):
                      def prod256(v, ic4=ic4, ot=ot):
                          sl = slice(ic4*1024, (ic4+1)*1024)
                          v.memset(y_whr[:, sl], 0.0)
                          v.drain()
                          for grp in range(16):
                              for j in range(16):
                                  c = grp * 16 + j
                                  v.tensor_scalar(out=y_prod[:, j*1024:(j+1)*1024],
                                                  in0=y_idxf[:, sl],
                                                  scalar1=float(c),
                                                  scalar2=y_lut8[:, c:c+1],
                                                  op0=ALU.is_equal, op1=ALU.mult)
                              v.drain()
                              for st in (1, 2, 4, 8):
                                  for cc in range(0, 16, 2 * st):
                                      v.tensor_tensor(
                                          out=y_prod[:, cc*1024:(cc+1)*1024],
                                          in0=y_prod[:, cc*1024:(cc+1)*1024],
                                          in1=y_prod[:, (cc+st)*1024:(cc+st+1)*1024],
                                          op=ALU.add)
                                  v.drain()
                              v.tensor_tensor(out=y_whr[:, sl], in0=y_whr[:, sl],
                                              in1=y_prod[:, 0:1024], op=ALU.add)
                              v.drain()
                      op("vector", prod256, waits=[("yv", (ty, "idx", ot, 3))],
                         inc="yv", key=(ty, "red", ot, 2 * ic4))
                      if k.counting:
                          ev[(ty, "red", ot, 2 * ic4 + 1)] = ev[(ty, "red", ot, 2 * ic4)]
                  if yh_level < 23:
                      continue
                  for icq in range(8):
                      bank = p_tr if icq % 2 == 0 else p_tr2
                      for i in range(4):
                          ic = icq * 4 + i
                          w = []
                          if i == 0:
                              w.append(("yv", (ty, "red", ot, (ic * 128) // 512)))
                              w.append(("yv", (ty, "red", ot, min(7, ((ic + 3) * 128) // 512))))
                              if icq >= 2:
                                  w.append(("ys", (ty, "trev", ot, icq - 2)))
                              if ot == 0 and icq < 2:
                                  w.append(("v", (t6, "addv", 7, 7)))
                          op("tensor", lambda t, bank=bank, i=i, ic=ic: t.transpose(
                              bank[:, i*128:(i+1)*128], y_whr[:, ic*128:(ic+1)*128], ident),
                             waits=w, inc="yt",
                             key=((ty, "trp", ot, icq) if i == 3 else None))
                      op("scalar", lambda s, bank=bank, icq=icq: s.copy(
                          out=y_pack[icq % 2], in_=bank[:]),
                         waits=[("yt", (ty, "trp", ot, icq)),
                                ("EV", (ty, "wst", ot, icq - 2))],
                         inc="ys", key=(ty, "trev", ot, icq))
                      ygdma(agh_in[icq*4:(icq+1)*4, ot]
                            .rearrange("c p o -> p c o"),
                            y_pack[icq % 2].rearrange("p (c o) -> p c o", o=128),
                            waits=[("ys", (ty, "trev", ot, icq))],
                            key=(ty, "wst", ot, icq))
            if k.counting:
                m0 = k.ctr["yv"]
                dequant_loop()
                ev["yvd"] = k.ctr["yv"] - m0
            elif k.engine == "vector":
                k.eng.wait_ge(k.sems[k.evsem["fli"]], ev["fli"])
                vregs = nc.alloc_registers("yfr_v", engines=[k.eng.engine])
                k.eng.reg_load(vregs[k.eng.engine], flagi2[0:1, 0:1])
                with nc.If_ne(vregs, 0):
                    dequant_loop()
                with nc.Else():
                    k.eng.sem_inc(k.sems["yv"], ev["yvd"])
            else:
                dequant_loop()
            if "yq8" in dbg:
                op("gpsimd", lambda g: g.dma_start(out=dbg["yq8"], in_=y_q8),
                   waits=[("EV", (ty, "qld", 3, p)) for p in range(8)] +
                         [("yv", (ty, "idx", 3, 3))], inc="gd", inc_n=16)
            if "yidx" in dbg:
                op("gpsimd", lambda g: g.dma_start(out=dbg["yidx"], in_=y_idxf),
                   waits=[("yv", (ty, "idx", 3, 3))], inc="gd", inc_n=16)
            if "ywhr" in dbg and yh_level >= 22:
                op("gpsimd", lambda g: g.dma_start(out=dbg["ywhr"], in_=y_whr),
                   waits=[("yv", (ty, "red", 3, 6))], inc="gd", inc_n=16)
            if "ylut" in dbg:
                op("gpsimd", lambda g: g.dma_start(out=dbg["ylut"], in_=y_lut8),
                   waits=[("yv", (ty, "red", 3, 6))], inc="gd", inc_n=16)
            if k.counting:
                for i in range(4):
                    ev[("yag", i)] = k.ctr[f"YG{i}"]
            if yh_level < 25:
                op("gpsimd", lambda g: None,
                   waits=[(f"YG{i}", ("yag", i)) for i in range(4)])
                return
            if "ywh" in dbg:
                op("gpsimd", lambda g: g.dma_start(out=dbg["ywh"], in_=agh_in),
                   waits=[(f"YG{i}", ("yag", i)) for i in range(4)], inc="gd", inc_n=16)
            op("gpsimd", lambda g: g.collective_compute(
                "AllGather", ALU.bypass, replica_groups=[list(range(8))],
                ins=[agh_in], outs=[agh_out]),
               waits=[(f"YG{i}", ("yag", i)) for i in range(4)], inc="ycc", key="ag3")
            if yh_level < 30:
                return
            ywt = [0]
            def ymm_loop():
              for oc in range(32):
                  gsh, csh = oc // 4, (oc % 4) * 128
                  pmb = p_mm if oc % 2 == 0 else p_s
                  lds = []
                  for ic in range(32):
                      n = ywt[0]; ywt[0] += 1
                      wi = n % 12
                      w = [("yt", (ty, "wfree", n - 12))] if n >= 12 else [("ycc", "ag3")]
                      ydma(wt[wi], agh_out[gsh, ic, csh // 128],
                           waits=w, key=(ty, "wld", n))
                      lds.append((n, wi))
                  for ic in range(32):
                      n, wi = lds[ic]
                      w = [("EV", (ty, "wld", n))]
                      if ic == 0:
                          w.append(("yv", (ty, "dfx", oc - 2, 1)))
                      op("tensor", lambda t, pmb=pmb, wi=wi, ic=ic: t.matmul(
                          pmb[0][:], wt[wi], ln2t[:, ic*1024: ic*1024+512],
                          start=(ic == 0), stop=(ic == 31)),
                         waits=w, inc="yt")
                      op("tensor", lambda t, pmb=pmb, wi=wi, ic=ic: t.matmul(
                          pmb[1][:], wt[wi], ln2t[:, ic*1024+512: ic*1024+1024],
                          start=(ic == 0), stop=(ic == 31)),
                         inc="yt", key=(ty, "wfree", n))
                      if k.counting:
                          ev[(ty, "ymm", oc, ic)] = ev[(ty, "wfree", n)]
                  for half in range(2):
                      def dfx(v, oc=oc, half=half, pmb=pmb):
                          v.tensor_scalar(out=y_dfix[:, half*512:(half+1)*512],
                                          in0=pmb[half][:], scalar1=fb_sb[:, oc:oc+1],
                                          scalar2=0.0, op0=ALU.add, op1=ALU.add)
                          v.drain()
                      wd = [("yt", (ty, "ymm", oc, 31))]
                      if oc > 0:
                          wd.append(("yt", (ty, "mtr", oc - 1, 7)))
                      op("vector", dfx, waits=wd, inc="yv", key=(ty, "dfx", oc, half))
                  for tokg in range(8):
                      bank = p_tr if tokg % 2 == 0 else p_tr2
                      w = [("yv", (ty, "dfx", oc, tokg // 4)),
                           ("yv", (ty, "dfx", oc, 1)),
                           ("yv", (ty, "mrg", oc, tokg - 2)),
                           ("yv", (ty, "mrg", oc - 1, 7))]
                      op("tensor", lambda t, bank=bank, tokg=tokg: t.transpose(
                          bank[:, 0:128], y_dfix[:, tokg*128:(tokg+1)*128], ident),
                         waits=w, inc="yt", key=(ty, "mtr", oc, tokg))
                      if tokg > 0:
                          wold = [("yv", (ty, "mrg", oc, tokg - 1))]
                      elif oc > 0:
                          wold = [("yv", (ty, "mrg", oc - 1, 7))]
                      else:
                          wold = [(f"G{i}", ("yhg", i)) for i in range(8)]
                      ygdma(y_obuf, out[tokg*128:(tokg+1)*128, oc*128:(oc+1)*128],
                            waits=wold, key=(ty, "old", oc, tokg))
                      ygdma(y_xm, xmid_d[tokg*128:(tokg+1)*128, oc*128:(oc+1)*128],
                            waits=(), key=(ty, "xld", oc, tokg))
                      def mrg(v, bank=bank, tokg=tokg):
                          v.tensor_tensor(out=y_tmp, in0=bank[:, 0:128], in1=y_xm,
                                          op=ALU.add)
                          v.drain()
                          v.tensor_tensor(out=y_tmp, in0=y_tmp, in1=y_obuf,
                                          op=ALU.subtract)
                          v.drain()
                          v.tensor_scalar(out=y_tmp, in0=y_tmp,
                                          scalar1=mcol[:, tokg:tokg+1], scalar2=0.0,
                                          op0=ALU.mult, op1=ALU.add)
                          v.drain()
                          v.tensor_tensor(out=y_onew, in0=y_obuf, in1=y_tmp, op=ALU.add)
                          v.drain()
                      wm = [("yt", (ty, "mtr", oc, tokg)),
                            ("EV", (ty, "old", oc, tokg)),
                            ("EV", (ty, "xld", oc, tokg))]
                      if tokg > 0:
                          wm.append(("EV", (ty, "ost", oc, tokg - 1)))
                      elif oc > 0:
                          wm.append(("EV", (ty, "ost", oc - 1, 7)))
                      op("vector", mrg, waits=wm,
                         inc="yv", key=(ty, "mrg", oc, tokg))
                      ygdma(out[tokg*128:(tokg+1)*128, oc*128:(oc+1)*128], y_onew,
                            waits=[("yv", (ty, "mrg", oc, tokg))],
                            key=(ty, "ost", oc, tokg))
            ymm_loop()
            if "ydfix" in dbg:
                op("gpsimd", lambda g: g.dma_start(out=dbg["ydfix"], in_=y_dfix),
                   waits=[("yv", (ty, "dfx", 31, 1))], inc="gd", inc_n=16)
            if "ypmm" in dbg:
                op("scalar", lambda s: s.copy(out=ostage[0], in_=p_mm[0][:]),
                   waits=[("yv", (ty, "dfx", 31, 1))], inc="ys", key=(ty, "pmmcp"))
                op("gpsimd", lambda g: g.dma_start(out=dbg["ypmm"], in_=ostage[0]),
                   waits=[("ys", (ty, "pmmcp"))], inc="gd", inc_n=16)
            if k.counting:
                for i in range(4):
                    ev[("yend", i)] = k.ctr[f"YG{i}"]
                    ev[("yendl", i)] = k.ctr[f"Y{i}"]
            op("gpsimd", lambda g: None,
               waits=[(f"YG{i}", ("yend", i)) for i in range(4)])
            op("sync", lambda e: None,
               waits=[(f"Y{i}", ("yendl", i)) for i in range(4)])

        if with_yh:
            slow_section()

        return finish(k)

    k.start_pass()
    program(k)
    with nc.Block() as blk:
        @blk.sync
        def _(e):
            k.start_pass("sync", e); program(k)
        @blk.vector
        def _(e):
            k.start_pass("vector", e); program(k)
        @blk.scalar
        def _(e):
            k.start_pass("scalar", e); program(k)
        @blk.tensor
        def _(e):
            k.start_pass("tensor", e); program(k)
        @blk.gpsimd
        def _(e):
            k.start_pass("gpsimd", e); program(k)
    return nc, dbg


def prep_core_inputs(full, core):
    x = full["L0_attn_hidden_states"]
    b = core // 4
    j = core % 4
    qendA = 512 * (j + 1)
    qendB = 512 * (8 - j)
    xbf = x[b]
    def slab(qend, slen):
        lo = qend - slen
        pad = max(0, -lo)
        seg = xbf[max(0, lo):qend]
        if pad:
            seg = np.concatenate([np.zeros((pad, D), np.float32), seg], axis=0)
        return np.ascontiguousarray(seg)
    def jmask(qend, slen):
        nch = slen // 128
        jm = np.zeros((128, nch), np.float32)
        npad = max(0, slen - qend)
        for ch in range(nch):
            pp = ch * 128 + np.arange(128)
            jm[:, ch] = np.where(pp < npad, NEG, 0.0)
        return jm
    r0 = core * 512
    return {
        "xa": slab(qendA, SLAB_A),
        "xb": slab(qendB, SLAB_B),
        "wqkv_sh": np.ascontiguousarray(full["L0_attn_Wqkv"][r0:r0+512]),
        "wo_sh": np.ascontiguousarray(full["L0_attn_Wo"][r0:r0+512]),
        "qw_sh": np.ascontiguousarray(full["L0_ffn_qweight"][:, r0:r0+512, :]),
        "lut4_sh": np.ascontiguousarray(full["L0_ffn_lut4"][r0:r0+512]),
        "lut8_sh": np.ascontiguousarray(full["L0_ffn_lut8"][r0:r0+512]),
        "fbias": np.ascontiguousarray(full["L0_ffn_bias"].reshape(32, 128).T),
        "jma": jmask(qendA, SLAB_A),
        "jmb": jmask(qendB, SLAB_B),
    }


def assemble_out(results):
    y = np.zeros((2, S, D), np.float32)
    for core in range(8):
        b, j = core // 4, core % 4
        o = results[core]["out"]
        y[b, 512*j:512*(j+1)] = o[:512]
        y[b, 512*(7-j):512*(8-j)] = o[512:]
    return y


_BUILD_CACHE = {}

def kernel(**inputs):
    key = "v7"
    if key not in _BUILD_CACHE:
        _BUILD_CACHE[key] = build(upto="P6")
    nc, _ = _BUILD_CACHE[key]
    from concourse.bass_utils import run_bass_kernel_spmd
    in_maps = [prep_core_inputs(inputs, c) for c in range(8)]
    res = run_bass_kernel_spmd(nc, in_maps, core_ids=list(range(8)))
    return assemble_out([res.results[c] for c in range(8)])
